# version 03
# baseline (speedup 1.0000x reference)
import numpy as np
import ml_dtypes
from contextlib import ExitStack

import concourse.bass as bass
import concourse.mybir as mybir
from concourse.bass_utils import run_bass_kernel_spmd

F32 = mybir.dt.float32
BF16 = mybir.dt.bfloat16
U8 = mybir.dt.uint8
AF = mybir.ActivationFunctionType
ALU = mybir.AluOpType
AX = mybir.AxisListType

D = 2048
SEQ = 2048
BATCH = 4
NCORES = 8
TOWN = 1024
HC = 16
NHEAD = 32
NE = 16
I_FF = 1408
SI = 2816
EXP_SHIFT = 12.0
LN_EPS = 1e-5

EVENKV = [h for h in range(NHEAD) if (h // 4) % 2 == 0]
ODDKV = [h for h in range(NHEAD) if (h // 4) % 2 == 1]
PERM_M = np.zeros(D, np.int64)
for _c in range(16):
    for _s, _h in enumerate((EVENKV[_c], ODDKV[_c])):
        PERM_M[128 * _c + 64 * _s: 128 * _c + 64 * _s + 64] = _h * 64 + np.arange(64)

BLK_NSLOT = [12, 16]


def _mask_idx(blk, tk):
    if blk == 0:
        return tk
    return 12 + (tk - 4) if tk >= 4 else None


ENGS = ("sync", "gpsimd", "tensor", "vector", "scalar")


class Sem:
    def __init__(self, handle):
        self.h = handle
        self.val = 0

    def inc(self, n=1):
        self.val += n
        return self.val

    def t(self):
        return (self, self.val)


class Prog:
    def __init__(self, nc):
        self.nc = nc
        self.streams = {k: [] for k in ENGS}

    def em(self, engine, fn):
        self.streams[engine].append(fn)

    def flush(self):
        with self.nc.Block() as B:
            for name in ENGS:
                fns = self.streams[name]
                getattr(B, name)(lambda eng, fns=fns: [f(eng) for f in fns])


def _w(eng, waits):
    seen = {}
    order = []
    for tk in waits:
        if tk is None:
            continue
        s, v = tk
        if s.h not in seen or seen[s.h] < v:
            if s.h not in seen:
                order.append(s.h)
            seen[s.h] = max(seen.get(s.h, 0), v)
    for h in order:
        eng.wait_ge(h, seen[h])


def build_program(upto="E", debug=False):
    nc = bass.Bass("TRN2", target_bir_lowering=False)
    P = Prog(nc)
    es = ExitStack()

    def din(name, shape, dtype=F32):
        return nc.dram_tensor(name, list(shape), dtype, kind="ExternalInput").ap()

    xT = din("xT", [D, SEQ])
    xTo = din("xTo", [D, TOWN])
    wqT = din("wqT", [D, D])
    bq2 = din("bq2", [128, HC])
    wkT = din("wkT", [D, 512])
    bk2 = din("bk2", [128, 4])
    wvT = din("wvT", [D, 512])
    bv = din("bv", [512])
    woT = din("woT", [D, D])
    bo2 = din("bo2", [128, HC])
    gateT = din("gateT", [D, NE])
    wgu = din("wgu", [NE, D, 2 * I_FF])
    wdn = din("wdn", [NE, I_FF, D])
    sgu = din("sgu", [D, 2 * SI])
    sdn = din("sdn", [SI, D])
    cosk = din("cosk", [128, SEQ])
    sink = din("sink", [128, SEQ])
    cosq = din("cosq", [128, TOWN])
    sinq = din("sinq", [128, TOWN])
    masksI = din("masks", [24, 128, 512], BF16)
    ones1 = din("ones1", [1, 128])
    ones1b = din("ones1b", [1, 128], BF16)
    onesr = din("onesr", [1, 512], BF16)
    e0m = din("e0m", [128, 128], BF16)
    onec1 = din("onec1", [128, 1])
    onec1b = din("onec1b", [128, 1], BF16)
    identf = din("identf", [128, 128])
    identb = din("identb", [128, 128], BF16)
    onehotI = din("onehot", [128, 2048], BF16)
    out = nc.dram_tensor("out", [D, TOWN], F32, kind="ExternalOutput").ap()
    x2s = nc.dram_tensor("x2s", [D, TOWN], F32, kind="Internal").ap()

    dbg = {}
    if debug:
        dbg["d_rsc"] = nc.dram_tensor("d_rsc", [128, 8, 16], F32, kind="ExternalOutput").ap()
        dbg["d_rcb"] = nc.dram_tensor("d_rcb", [128, 8, 16], F32, kind="ExternalOutput").ap()
        dbg["d_lbc"] = nc.dram_tensor("d_lbc", [128, 2, 512], F32, kind="ExternalOutput").ap()
        dbg["d_lrb"] = nc.dram_tensor("d_lrb", [1, 2, 512], F32, kind="ExternalOutput").ap()
        dbg["d_lst"] = nc.dram_tensor("d_lst", [1, 4, 512], F32, kind="ExternalOutput").ap()
        dbg["d_yatt"] = nc.dram_tensor("d_yatt", [D, 512], F32, kind="ExternalOutput").ap()
        for name, shape in [("d_k", [512, SEQ]), ("d_v", [SEQ, 512]),
                            ("d_q", [D, TOWN]), ("d_o", [D, TOWN]),
                            ("d_x2", [D, TOWN]), ("d_ln2", [D, TOWN]),
                            ("d_comb", [NE, TOWN]), ("d_y", [D, TOWN])]:
            dbg[name] = nc.dram_tensor(name, shape, F32, kind="ExternalOutput").ap()

    ARENA = 212224
    arena = es.enter_context(nc.sbuf_tensor("arena", [128, ARENA], U8))
    base = nc.lookup_mloc(arena).addr

    def at(name, shape, dtype, off, end):
        nbytes = int(np.prod(shape[1:])) * mybir.dt.size(dtype)
        assert off + nbytes <= end, f"{name}: {off}+{nbytes} > {end}"
        h = nc.alloc_sbuf_tensor_at(name, list(shape), dtype, offset=base + off)
        return h.ap()

    c_ones = at("c_ones", [1, 128], F32, 0, 16384)
    c_onesb = at("c_onesb", [1, 128], BF16, 512, 16384)
    c_id = at("c_id", [128, 128], F32, 768, 16384)
    c_idb = at("c_idb", [128, 128], BF16, 14464, 16384)
    c_oneh = at("c_oneh", [128, 2048], BF16, 1280, 16384)
    c_gateT = at("c_gateT", [128, HC, NE], BF16, 9472, 16384)
    c_bq = at("c_bq", [128, HC], F32, 9984, 16384)
    c_bk = at("c_bk", [128, 4], F32, 10048, 16384)
    c_bo = at("c_bo", [128, HC], F32, 10080, 16384)
    c_bvb = at("c_bvb", [128, 512], F32, 10144, 16384)
    c_negC = at("c_negC", [128, 1], F32, 12192, 16384)
    c_onec = at("c_onec", [128, 1], F32, 14336, 16384)
    c_onecb = at("c_onecb", [128, 1], BF16, 14400, 16384)
    c_onesr = at("c_onesr", [1, 512], BF16, 13280, 16384)
    c_e0 = at("c_e0", [128, 128], BF16, 15744, 16384)
    bv_row = at("bv_row", [1, 512], F32, 12224, 16384)
    bv_rowb = at("bv_rowb", [128, 512], BF16, 14720, 16384)
    l_rb = at("l_rb", [128, 2, 512], BF16, 38912, 43008)
    l_sq = at("l_sq", [128, 2, 512], BF16, 16384, 43008)
    l_xb = at("l_xb", [128, 2, 512], BF16, 18432, 43008)
    l_nrm = at("l_nrm", [128, 512], F32, 20480, 43008)
    l_stat = at("l_stat", [1, 4, 512], F32, 22528, 43008)
    l_bc = at("l_bc", [128, 2, 512], F32, 30720, 43008)
    l_bc2 = at("l_bc2", [128, 2, 512], F32, 34816, 43008)
    wpool = at("wpool", [128, 4, 16, 128], BF16, 43008, 76800)
    dpool = at("dpool", [128, 2, 11, 256], BF16, 59392, 76800)

    AB = 76800

    k_sb = at("k_sb", [128, 4, SEQ], BF16, AB, 212224)
    v_sb = at("v_sb", [128, HC, 8 * 65], BF16, 93184, 212224)
    xf = at("xf", [128, HC, 512], F32, 109824, 212224)
    lf = at("lf", [128, HC, 512], BF16, 142592, 212224)
    wk_sb = at("wk_sb", [128, HC, 512], BF16, 158976, 212224)
    wv_sb = at("wv_sb", [128, HC, 512], BF16, 175360, 212224)
    ckc = at("ckc", [128, 2, 512], F32, 191744, 212224)
    skc = at("skc", [128, 2, 512], F32, 195840, 212224)
    kraw = at("kraw", [128, 2, 512], F32, 199936, 212224)
    kswp = at("kswp", [128, 2, 512], F32, 204032, 212224)
    kt1 = at("kt1", [128, 512], F32, 208128, 212224)
    kt2 = at("kt2", [128, 512], F32, 210176, 212224)
    q_sb = at("q_sb", [128, HC, TOWN], BF16, 109824, 212224)
    ln1o = at("ln1o", [128, HC, 512], BF16, 142592, 212224)
    xo = at("xo", [128, HC, 512], F32, 158976, 212224)
    cq_sb = at("cq_sb", [128, TOWN], F32, 191744, 212224)
    sq_sb = at("sq_sb", [128, TOWN], F32, 195840, 212224)
    qraw = at("qraw", [128, 2, 512], F32, 199936, 212224)
    qswp = at("qswp", [128, 2, 512], F32, 204032, 212224)
    qt1 = at("qt1", [128, 512], F32, 208128, 212224)
    qt2 = at("qt2", [128, 512], F32, 210176, 212224)
    m_sb = at("m_sb", [128, 24, 512], BF16, 142592, 212224)
    pt = at("pt", [128, 8, 512], BF16, 167168, 212224)
    o_sb = at("o_sb", [128, HC, TOWN], BF16, 175360, 212224)
    rb = at("rb", [64, 2, 512], F32, 16384, 43008)
    rrow = at("rrow", [1, 2, 512], F32, 22528, 43008)
    rrowb = at("rrowb", [128, 2, 512], BF16, 26624, 43008)
    yatt = at("yatt", [128, HC, TOWN], F32, AB, 212224)
    ln2_sb = at("ln2_sb", [128, HC, TOWN], BF16, 142336, 212224)
    xoc = at("xoc", [128, 2, 512], F32, 208128, 212224)
    y_sb = at("y_sb", [128, HC, TOWN], BF16, AB, 212224)
    h_bu = [at("h_b0", [128, 11, TOWN], BF16, 109568, 212224),
            at("h_b1", [128, 11, TOWN], BF16, 175104, 212224)]
    st1 = at("st1", [128, TOWN], F32, 132096, 212224)
    st2 = at("st2", [128, TOWN], F32, 136192, 212224)
    r_sc = at("r_sc", [128, 2, NE], F32, 140288, 212224)
    r_m = at("r_m", [128, 4], F32, 140544, 212224)
    cb = at("cb", [128, 2, TOWN], F32, 197632, 212224)
    combT = at("combT", [128, TOWN], BF16, 205824, 212224)
    r_cb = at("r_cb", [128, NE], BF16, 209920, 212224)
    xo2 = at("xo2", [128, 2, 512], F32, 109568, 212224)
    ost = at("ost", [128, 2, 512], F32, 113664, 212224)

    def sem(name):
        return Sem(es.enter_context(nc.semaphore(name)))

    PSB = [es.enter_context(nc.psum_tensor(f"ps{i}", [128, 512], F32)) for i in range(8)]
    BANKT = [None] * 8

    qC = sem("qC")
    qG = sem("qG")
    qKV = sem("qKV")
    qM = sem("qM")
    qQt = sem("qQt")
    qXA = sem("qXA")
    qKt = [sem("qKt0"), sem("qKt1")]
    qKS = [sem("qKS0"), sem("qKS1")]
    qQS = [sem("qQS0"), sem("qQS1")]
    qWs = [sem("qWs0"), sem("qWs1")]
    qD = [sem("qD0"), sem("qD1")]
    qXC = [sem("qXC0"), sem("qXC1")]
    qR = [sem("qR0"), sem("qR1")]
    qO = [sem("qO0"), sem("qO1")]
    qSp = sem("qSp")
    qDBG = sem("qDBG")
    sT = sem("sT")
    sV = sem("sV")
    sS = sem("sS")
    mG = sem("mG")
    mY = sem("mY")
    bT = sem("bT")
    bV = sem("bV")
    bS = sem("bS")
    MIL = {"sync": mY, "gpsimd": mG, "tensor": bT, "vector": bV, "scalar": bS}

    DATA = {"tensor": sT, "vector": sV, "scalar": sS}

    def barrier():
        vals = {e: MIL[e].inc() for e in ENGS}
        for e in ENGS:
            others = [(MIL[o], vals[o]) for o in ENGS if o != e]
            selfw = (DATA[e], DATA[e].val) if e in DATA else None

            def _(eng, own=MIL[e], others=others, selfw=selfw):
                if selfw is not None:
                    _w(eng, [selfw])
                eng.nop().then_inc(own.h, 1)
                _w(eng, others)
            P.em(e, _)

    def _(sy):
        for dst, srcap in [(c_ones, ones1[:, :]), (c_onesb, ones1b[:, :]),
                           (c_onec, onec1[:, :]), (c_onecb, onec1b[:, :]),
                           (c_id, identf[:, :]), (c_idb, identb[:, :]),
                           (c_oneh, onehotI[:, :]),
                           (c_onesr, onesr[:, :]), (c_e0, e0m[:, :]),
                           (c_bq, bq2[:, :]), (c_bk, bk2[:, :]),
                           (c_bo, bo2[:, :]), (bv_row, bv[None, :])]:
            sy.dma_start(out=dst[:], in_=srcap).then_inc(qC.h, 16)
    P.em("sync", _)
    qC.inc(13 * 16)
    consts_ld = qC.t()

    def _(g):
        g.dma_start(out=c_gateT[:], in_=gateT.rearrange("(c p) e -> p c e", p=128)).then_inc(qG.h, 16)
    P.em("gpsimd", _)
    qG.inc(16)
    gate_ld = qG.t()

    def _(v, w=[consts_ld]):
        v.memset(c_negC[:], -float(EXP_SHIFT))
        v.memset(bv_rowb[:], 0.0)
        v.memset(l_rb[:, :, :], 0.0)
        _w(v, w)
        v.tensor_scalar_mul(c_bq[:], c_bq[:], 0.125).then_inc(sV.h, 1)
    P.em("vector", _)
    bq_ok = (sV, sV.inc())

    def _(s, w=[consts_ld]):
        _w(s, w)
        s.activation(out=bv_rowb[0:1, :], in_=bv_row[:], func=AF.Identity).then_inc(sS.h, 1)
    P.em("scalar", _)
    bvb_cast = (sS, sS.inc())

    def _(t, w=[bvb_cast, bq_ok]):
        _w(t, w)
        t.matmul(PSB[6][:], c_e0[:, :], bv_rowb[:, :], start=True, stop=True).then_inc(sT.h, 1)
    P.em("tensor", _)
    bvb_mm = (sT, sT.inc())

    def _(s, w=[bvb_mm]):
        _w(s, w)
        s.copy(out=c_bvb[:], in_=PSB[6][:]).then_inc(sS.h, 1)
    P.em("scalar", _)
    bvb_ok = (sS, sS.inc())
    BANKT[6] = bvb_ok

    def emit_ln_stats(src3, ready, bc_dst, banks, src_is_bf16=False,
                      extra_sq_ready=None):
        b0, b1, b2, b3 = banks
        sbase = sS.val

        def _(s, src3=src3, w=[ready, extra_sq_ready], tb=sT.val):
            _w(s, w)
            for hc in range(HC):
                if hc >= 1:
                    s.wait_ge(sT.h, max(tb + hc - 1, 0))
                s.activation(out=l_sq[:, hc % 2, :], in_=src3[:, hc, :], func=AF.Square)\
                    .then_inc(sS.h, 1)
        P.em("scalar", _)
        sS.inc(HC)

        if not src_is_bf16:
            vbase = sV.val

            def _(v, src3=src3, w=[ready], tb=sT.val):
                _w(v, w)
                for hc in range(HC):
                    if hc >= 1:
                        v.wait_ge(sT.h, max(tb + hc - 1, 0))
                    v.tensor_copy(l_xb[:, hc % 2, :], src3[:, hc, :]).then_inc(sV.h, 1)
            P.em("vector", _)
            sV.inc(HC)

        def _(t, src3=src3, w=[ready, BANKT[b0], BANKT[b1]], sbase=sbase,
              vbase=sV.val - HC, src_is_bf16=src_is_bf16):
            _w(t, w)
            for hc in range(HC):
                if src_is_bf16:
                    t.matmul(PSB[b0][0:1, :], c_onecb[:], src3[:, hc, :],
                             start=(hc == 0), stop=(hc == HC - 1))
                else:
                    t.wait_ge(sV.h, vbase + hc + 1)
                    t.matmul(PSB[b0][0:1, :], c_onecb[:], l_xb[:, hc % 2, :],
                             start=(hc == 0), stop=(hc == HC - 1))
                t.wait_ge(sS.h, sbase + hc + 1)
                t.matmul(PSB[b1][0:1, :], c_onecb[:], l_sq[:, hc % 2, :],
                         start=(hc == 0), stop=(hc == HC - 1)).then_inc(sT.h, 1)
        P.em("tensor", _)
        sT.inc(HC)
        stats_mm = sT.t()

        def _(v, w=[stats_mm], base=sV.val):
            _w(v, w)
            k = base
            for fn in [
                lambda: v.tensor_scalar_mul(l_stat[:, 0, :], PSB[b0][0:1, :], 1.0 / D),
                lambda: v.tensor_scalar_mul(l_stat[:, 1, :], PSB[b1][0:1, :], 1.0 / D),
                lambda: v.tensor_mul(l_stat[:, 2, :], l_stat[:, 0, :], l_stat[:, 0, :]),
                lambda: v.tensor_sub(l_stat[:, 1, :], l_stat[:, 1, :], l_stat[:, 2, :]),
                lambda: v.tensor_scalar_add(l_stat[:, 1, :], l_stat[:, 1, :], LN_EPS),
            ]:
                fn().then_inc(sV.h, 1)
                k += 1
                v.wait_ge(sV.h, k)
        P.em("vector", _)
        sV.inc(5)
        var_ok = sV.t()
        BANKT[b0] = var_ok
        BANKT[b1] = var_ok

        def _(s, w=[var_ok]):
            _w(s, w)
            s.activation(out=l_stat[:, 2, :], in_=l_stat[:, 1, :], func=AF.Sqrt)\
                .then_inc(sS.h, 1)
        P.em("scalar", _)
        sqrt_ok = (sS, sS.inc())

        def _(v, w=[sqrt_ok], base=sV.val):
            _w(v, w)
            v.reciprocal(l_stat[:, 2, :], l_stat[:, 2, :]).then_inc(sV.h, 1)
            v.wait_ge(sV.h, base + 1)
            v.tensor_mul(l_stat[:, 3, :], l_stat[:, 0, :], l_stat[:, 2, :])\
                .then_inc(sV.h, 1)
        P.em("vector", _)
        rmul_ok = (sV, sV.inc(2))
        rmul_ok = sV.t()

        def _(s, w=[rmul_ok]):
            _w(s, w)
            s.activation(out=l_rb[0:1, 0, :], in_=l_stat[:, 2, :], func=AF.Identity)
            s.activation(out=l_rb[0:1, 1, :], in_=l_stat[:, 3, :], func=AF.Identity)\
                .then_inc(sS.h, 1)
        P.em("scalar", _)
        rstd_ok = (sS, sS.inc())

        def _(t, w=[rstd_ok, BANKT[b2], BANKT[b3]]):
            _w(t, w)
            t.matmul(PSB[b2][:], c_e0[:, :], l_rb[:, 0, :], start=True, stop=True)
            t.matmul(PSB[b3][:], c_e0[:, :], l_rb[:, 1, :], start=True, stop=True)\
                .then_inc(sT.h, 1)
        P.em("tensor", _)
        bc_mm = (sT, sT.inc())

        def _(s, bc_dst=bc_dst, w=[bc_mm]):
            _w(s, w)
            s.copy(out=bc_dst[:, 0, :], in_=PSB[b2][:])
            s.copy(out=bc_dst[:, 1, :], in_=PSB[b3][:]).then_inc(sS.h, 1)
        P.em("scalar", _)
        bc_ok = (sS, sS.inc())
        BANKT[b2] = bc_ok
        BANKT[b3] = bc_ok
        return bc_ok

    run_A = upto not in ("P0",)
    run_A2 = upto not in ("P0", "A", "ALN", "AK", "AK1", "AK2", "AKE", "AKM", "AVM", "AVE", "AV")
    run_K = upto not in ("ALN", "AV")
    run_V = upto not in ("ALN", "AK", "AK1", "AK2", "AKE", "AKM")
    run_KSWP = upto not in ("AK1", "AKE", "AKM")
    run_KROPE = upto not in ("AK1", "AK2", "AKE", "AKM")
    def _(g):
        for part in range(4):
            g.dma_start(out=wk_sb[:, 4 * part:4 * part + 4, :],
                        in_=wkT.rearrange("(c p) n -> p c n", p=128)[:, 4 * part:4 * part + 4, :]).then_inc(qKV.h, 16)
            g.dma_start(out=wv_sb[:, 4 * part:4 * part + 4, :],
                        in_=wvT.rearrange("(c p) n -> p c n", p=128)[:, 4 * part:4 * part + 4, :]).then_inc(qKV.h, 16)
    P.em("gpsimd", _)
    qKV.inc(8 * 16)
    wkv_ld = qKV.t()

    def _(v, w=[bvb_ok]):
        _w(v, w)
        v.memset(v_sb[:, :, :], 1.0).then_inc(sV.h, 1)
    P.em("vector", _)
    vones_ok = (sV, sV.inc())

    chunk_free = None
    A_RANGE = range(4) if run_A else range(0)
    prevA = {"kevac": [None, None], "kswpdma": None, "kt1": None, "krope": [None, None],
             "vevac": [None, None], "ropeld": [None, None]}
    for tc in A_RANGE:
        def _(sy, tc=tc, w=[chunk_free]):
            _w(sy, w)
            for part in range(4):
                sy.dma_start(
                    out=xf[:, 4 * part:4 * part + 4, :],
                    in_=xT.rearrange("(c p) t -> p c t", p=128)
                        [:, 4 * part:4 * part + 4, 512 * tc:512 * tc + 512],
                ).then_inc(qXA.h, 16)
        P.em("sync", _)
        qXA.inc(64)
        x_ld = qXA.t()

        def _(sy, tc=tc, w=[prevA["ropeld"][tc % 2]]):
            _w(sy, w)
            sy.dma_start(out=ckc[:, tc % 2, :], in_=cosk[:, 512 * tc:512 * tc + 512]).then_inc(qKt[tc % 2].h, 16)
            sy.dma_start(out=skc[:, tc % 2, :], in_=sink[:, 512 * tc:512 * tc + 512]).then_inc(qKt[tc % 2].h, 16)
        P.em("sync", _)
        qKt[tc % 2].inc(32)
        ktab_ld = qKt[tc % 2].t()

        bc_ok = emit_ln_stats(xf, x_ld, l_bc, (0, 1, 2, 3))

        def _(v, w=[bc_ok]):
            _w(v, w)
            for hc in range(HC):
                v.tensor_mul(l_nrm[:], xf[:, hc, :], l_bc[:, 0, :])
                op = v.tensor_sub(lf[:, hc, :], l_nrm[:], l_bc[:, 1, :])
            op.then_inc(sV.h, 1)
        P.em("vector", _)
        ln_ok = (sV, sV.inc())

        for dc in (range(4) if run_K else range(0)):
            bk_ = 4

            def _(t, dc=dc, bk_=bk_, w=[ln_ok, wkv_ld, BANKT[bk_]]):
                _w(t, w)
                for hc in range(HC):
                    mm = t.matmul(PSB[bk_][:], wk_sb[:, hc, 128 * dc:128 * dc + 128],
                                  lf[:, hc, :], start=(hc == 0), stop=(hc == HC - 1))
                mm.then_inc(sT.h, 1)
            P.em("tensor", _)
            kmm = (sT, sT.inc())
            if upto == "AKM":
                prevA["kevac"][dc % 2] = kmm
                BANKT[bk_] = kmm
                prevA["krope"][dc % 2] = kmm
                prevA["kt1"] = kmm
                krope_ok = kmm
                continue

            def _(s, dc=dc, bk_=bk_,
                  w=[kmm, prevA["kswpdma"], prevA["kt1"]]):
                _w(s, w)
                if upto == "AKE":
                    s.copy(out=kraw[:, dc % 2, :], in_=PSB[bk_][:])
                    s.nop().then_inc(sS.h, 1)
                else:
                    s.activation(out=kraw[:, dc % 2, :], in_=PSB[bk_][:], func=AF.Identity,
                                 bias=c_bk[:, dc:dc + 1], scale=1.0).then_inc(sS.h, 1)
            P.em("scalar", _)
            kevac = (sS, sS.inc())
            prevA["kevac"][dc % 2] = kevac
            BANKT[bk_] = kevac

            if run_KSWP:
                def _(sy, dc=dc, w=[kevac, prevA["krope"][dc % 2]]):
                    _w(sy, w)
                    p = dc % 2
                    sy.dma_start(out=kswp[0:32, p, :], in_=kraw[32:64, p, :]).then_inc(qKS[p].h, 16)
                    sy.dma_start(out=kswp[32:64, p, :], in_=kraw[0:32, p, :]).then_inc(qKS[p].h, 16)
                    sy.dma_start(out=kswp[64:96, p, :], in_=kraw[96:128, p, :]).then_inc(qKS[p].h, 16)
                    sy.dma_start(out=kswp[96:128, p, :], in_=kraw[64:96, p, :]).then_inc(qKS[p].h, 16)
                P.em("sync", _)
                qKS[dc % 2].inc(64)
                swp_ok = qKS[dc % 2].t()
                prevA["kswpdma"] = swp_ok
            else:
                swp_ok = None

            if run_KROPE:
                def _(v, tc=tc, dc=dc, w=[kevac, ktab_ld], w2=swp_ok):
                    p = dc % 2
                    _w(v, w)
                    v.tensor_mul(kt1[:], kraw[:, p, :], ckc[:, tc % 2, :]).then_inc(sV.h, 1)
                    _w(v, [w2])
                    v.tensor_mul(kt2[:], kswp[:, p, :], skc[:, tc % 2, :])
                    v.tensor_add(k_sb[:, dc, 512 * tc:512 * tc + 512], kt1[:], kt2[:])\
                        .then_inc(sV.h, 1)
                P.em("vector", _)
                sV.inc()
                prevA["kt1"] = sV.t()
                krope_ok = (sV, sV.inc())
                prevA["krope"][dc % 2] = krope_ok
            else:
                def _(v, tc=tc, dc=dc, w=[kevac] + ([swp_ok] if swp_ok else [])):
                    p = dc % 2
                    _w(v, w)
                    v.tensor_copy(k_sb[:, dc, 512 * tc:512 * tc + 512], kraw[:, p, :])
                    v.nop().then_inc(sV.h, 1).then_inc(sV.h, 0) if False else None
                    v.nop().then_inc(sV.h, 1)
                P.em("vector", _)
                krope_ok = (sV, sV.inc())
                prevA["kt1"] = krope_ok
                prevA["krope"][dc % 2] = krope_ok
        if run_K:
            prevA["ropeld"][tc % 2] = krope_ok

        for ts in (range(4) if run_V else range(0)):
            bk_ = 5

            def _(t, ts=ts, bk_=bk_, w=[ln_ok, wkv_ld, BANKT[bk_]]):
                _w(t, w)
                for hc in range(HC):
                    mm = t.matmul(PSB[bk_][:], lf[:, hc, 128 * ts:128 * ts + 128],
                                  wv_sb[:, hc, :], start=(hc == 0), stop=(hc == HC - 1))
                mm.then_inc(sT.h, 1)
            P.em("tensor", _)
            vmm = (sT, sT.inc())

            if upto == "AVM":
                def _(v, w=[vmm]):
                    _w(v, w)
                    v.nop().then_inc(sV.h, 1)
                P.em("vector", _)
            elif upto == "AVE":
                def _(v, tc=tc, ts=ts, bk_=bk_, w=[vmm, vones_ok]):
                    _w(v, w)
                    for h8 in range(8):
                        v.tensor_add(v_sb[:, 4 * tc + ts, 65 * h8:65 * h8 + 64],
                                     PSB[bk_][:, 64 * h8:64 * h8 + 64],
                                     c_bvb[:, 64 * h8:64 * h8 + 64])
                    v.nop().then_inc(sV.h, 1)
                P.em("vector", _)
            else:
                def _(v, tc=tc, ts=ts, bk_=bk_, w=[vmm, vones_ok]):
                    _w(v, w)
                    v.tensor_add(
                        v_sb[:, 4 * tc + ts, :].rearrange("p (h d) -> p h d", d=65)[:, :, 0:64],
                        PSB[bk_][:].rearrange("p (h d) -> p h d", d=64),
                        c_bvb[:].rearrange("p (h d) -> p h d", d=64),
                    ).then_inc(sV.h, 1)
                P.em("vector", _)
            vevac = (sV, sV.inc())
            BANKT[bk_] = vevac

        chunk_free = sT.t()

    def _(sy):
        sy.dma_start(out=cq_sb[:], in_=cosq[:, :]).then_inc(qQt.h, 16)
        sy.dma_start(out=sq_sb[:], in_=sinq[:, :]).then_inc(qQt.h, 16)
    P.em("sync", _)
    qQt.inc(32)
    qtab_ld = qQt.t()

    qmm_hist = []
    nq = 0
    TH_RANGE = range(2) if run_A2 else range(0)
    prevQ = {"evac": [None, None], "swpdma": None, "qt1": None, "qrope": [None, None]}
    last_chunk_users = chunk_free
    for th in TH_RANGE:
        def _(sy, th=th, w=[last_chunk_users]):
            _w(sy, w)
            for part in range(4):
                sy.dma_start(
                    out=xo[:, 4 * part:4 * part + 4, :],
                    in_=xTo.rearrange("(c p) t -> p c t", p=128)
                        [:, 4 * part:4 * part + 4, 512 * th:512 * th + 512],
                ).then_inc(qXA.h, 16)
        P.em("sync", _)
        qXA.inc(64)
        xo_ld = qXA.t()

        bc_ok = emit_ln_stats(xo, xo_ld, l_bc, (0, 1, 2, 3))

        def _(v, w=[bc_ok]):
            _w(v, w)
            for hc in range(HC):
                v.tensor_mul(l_nrm[:], xo[:, hc, :], l_bc[:, 0, :])
                op = v.tensor_sub(ln1o[:, hc, :], l_nrm[:], l_bc[:, 1, :])
            op.then_inc(sV.h, 1)
        P.em("vector", _)
        lno_ok = (sV, sV.inc())

        for jc in range(HC):
            buf = nq % 2
            bk_ = 4 + nq % 2

            def _(g, th=th, jc=jc, buf=buf, nq=nq,
                  wfree=(qmm_hist[nq - 2] if nq >= 2 else None)):
                if wfree is not None:
                    _w(g, [wfree])
                g.dma_start(
                    out=wpool[:, buf, 0:HC, :],
                    in_=wqT.rearrange("(c p) j -> p c j", p=128)[:, :, 128 * jc:128 * jc + 128],
                ).then_inc(qWs[nq % 2].h, 16)
            P.em("gpsimd", _)
            qWs[nq % 2].inc(16)
            wq_ld = qWs[nq % 2].t()

            def _(t, th=th, jc=jc, buf=buf, bk_=bk_,
                  w=[wq_ld, lno_ok, BANKT[bk_], bq_ok]):
                _w(t, w)
                for hc in range(HC):
                    mm = t.matmul(PSB[bk_][:], wpool[:, buf, hc, :], ln1o[:, hc, :],
                                  start=(hc == 0), stop=(hc == HC - 1))
                mm.then_inc(sT.h, 1)
            P.em("tensor", _)
            qmm = (sT, sT.inc())
            qmm_hist.append(qmm)

            def _(s, jc=jc, bk_=bk_, p=nq % 2,
                  w=[qmm, prevQ["swpdma"], prevQ["qt1"]]):
                _w(s, w)
                s.activation(out=qraw[:, p, :], in_=PSB[bk_][:], func=AF.Identity,
                             bias=c_bq[:, jc:jc + 1], scale=0.125).then_inc(sS.h, 1)
            P.em("scalar", _)
            qevac = (sS, sS.inc())
            BANKT[bk_] = qevac

            def _(sy, p=nq % 2, w=[qevac, prevQ["qrope"][nq % 2]]):
                _w(sy, w)
                sy.dma_start(out=qswp[0:32, p, :], in_=qraw[32:64, p, :]).then_inc(qQS[p].h, 16)
                sy.dma_start(out=qswp[32:64, p, :], in_=qraw[0:32, p, :]).then_inc(qQS[p].h, 16)
                sy.dma_start(out=qswp[64:96, p, :], in_=qraw[96:128, p, :]).then_inc(qQS[p].h, 16)
                sy.dma_start(out=qswp[96:128, p, :], in_=qraw[64:96, p, :]).then_inc(qQS[p].h, 16)
            P.em("sync", _)
            qQS[nq % 2].inc(64)
            swp_ok = qQS[nq % 2].t()
            prevQ["swpdma"] = swp_ok

            def _(v, th=th, jc=jc, p=nq % 2, w=[qevac, qtab_ld], w2=swp_ok):
                _w(v, w)
                v.tensor_mul(qt1[:], qraw[:, p, :], cq_sb[:, 512 * th:512 * th + 512])\
                    .then_inc(sV.h, 1)
                _w(v, [w2])
                v.tensor_mul(qt2[:], qswp[:, p, :], sq_sb[:, 512 * th:512 * th + 512])
                v.tensor_add(q_sb[:, jc, 512 * th:512 * th + 512], qt1[:], qt2[:])\
                    .then_inc(sV.h, 1)
            P.em("vector", _)
            sV.inc()
            prevQ["qt1"] = sV.t()
            qrope_ok = (sV, sV.inc())
            prevQ["qrope"][nq % 2] = qrope_ok
            nq += 1
        last_chunk_users = sT.t()

    barrier()

    if upto in ("B", "C", "E", "DR", "DR1", "CW"):
        def _(sy):
            for g4 in range(4):
                sy.dma_start(out=m_sb[:, 6 * g4:6 * g4 + 6, :],
                             in_=masksI.rearrange("s p t -> p s t")[:, 6 * g4:6 * g4 + 6, :]).then_inc(qM.h, 16)
        P.em("sync", _)
        qM.inc(64)
        mask_ld = qM.t()

        def _(v):
            v.memset(rrowb[:, :, :], 0.0).then_inc(sV.h, 1)
        P.em("vector", _)
        sV.inc()

        pt_owner = [None] * 8
        rrow_guard = [None, None]
        ptc = 0
        for blk in range(2):
            nslot = BLK_NSLOT[blk]
            for cp in range(16):
                hA, hB = EVENKV[cp], ODDKV[cp]
                khA, khB = hA // 4, hB // 4
                pts = {}
                msk_ok = {}

                def emit_s_exp(si, hs, kh):
                    nonlocal ptc
                    bk_ = hs * 2 + si % 2
                    base = (kh % 2) * 64

                    def _(t, blk=blk, cp=cp, si=si, kh=kh, base=base, bk_=bk_,
                          w=[BANKT[bk_], mask_ld]):
                        _w(t, w)
                        t.matmul(PSB[bk_][:],
                                 k_sb[base:base + 64, kh // 2, 128 * si:128 * si + 128],
                                 q_sb[base:base + 64, cp, 512 * blk:512 * blk + 512],
                                 start=True, stop=True).then_inc(sT.h, 1)
                    P.em("tensor", _)
                    smm = (sT, sT.inc())

                    slot = ptc % 8
                    ptc += 1

                    def _(s, bk_=bk_, slot=slot, w=[smm, pt_owner[slot]]):
                        _w(s, w)
                        s.activation(out=pt[:, slot, :], in_=PSB[bk_][:], func=AF.Exp,
                                     bias=c_negC[:], scale=1.0).then_inc(sS.h, 1)
                    P.em("scalar", _)
                    e_ok = (sS, sS.inc())
                    BANKT[bk_] = e_ok

                    mi = _mask_idx(blk, si)
                    if mi is not None:
                        def _(v, slot=slot, mi=mi, w=[e_ok]):
                            _w(v, w)
                            v.tensor_mul(pt[:, slot, :], pt[:, slot, :],
                                         m_sb[:, mi, :]).then_inc(sV.h, 1)
                        P.em("vector", _)
                        msk_ok[(si, hs)] = (sV, sV.inc())
                    else:
                        msk_ok[(si, hs)] = e_ok
                    pts[(si, hs)] = slot

                def emit_av(sj, hs, kh):
                    slot = pts[(sj, hs)]
                    obk = 4 + hs

                    def _(t, sj=sj, kh=kh, slot=slot, obk=obk, nslot=nslot,
                          w=[msk_ok[(sj, hs)]] + ([BANKT[obk]] if sj == 0 else [])):
                        _w(t, w)
                        t.matmul(PSB[obk][0:65, :], v_sb[:, sj, 65 * kh:65 * kh + 65],
                                 pt[:, slot, :], start=(sj == 0), stop=(sj == nslot - 1))\
                            .then_inc(sT.h, 1)
                    P.em("tensor", _)
                    av = (sT, sT.inc())
                    pt_owner[slot] = av
                    return av

                av_last = {}
                for si in range(nslot):
                    emit_s_exp(si, 0, khA)
                    emit_s_exp(si, 1, khB)
                    if si >= 1:
                        av_last[0] = emit_av(si - 1, 0, khA)
                        av_last[1] = emit_av(si - 1, 1, khB)
                av_last[0] = emit_av(nslot - 1, 0, khA)
                av_last[1] = emit_av(nslot - 1, 1, khB)

                for hs, kh in ((0, khA), (1, khB)):
                    obk = 4 + hs

                    def _(v, hs=hs, obk=obk, w=[av_last[hs], rrow_guard[hs]],
                          base=sV.val):
                        _w(v, w)
                        v.reciprocal(rrow[:, hs, :], PSB[obk][64:65, :]).then_inc(sV.h, 1)
                        v.wait_ge(sV.h, base + 1)
                        v.tensor_copy(rrowb[0:1, hs, :], rrow[:, hs, :]).then_inc(sV.h, 1)
                    P.em("vector", _)
                    sV.inc(2)
                    r_ok = sV.t()

                    def _(t, hs=hs, w=[r_ok, BANKT[6]]):
                        _w(t, w)
                        t.matmul(PSB[6][64 * hs:64 * hs + 64, :], c_e0[:, 0:64],
                                 rrowb[:, hs, :], start=True, stop=True).then_inc(sT.h, 1)
                    P.em("tensor", _)
                    rb_mm = (sT, sT.inc())
                    rrow_guard[hs] = rb_mm

                    def _(s, hs=hs, w=[rb_mm, BANKT[4 + hs]]):
                        _w(s, w)
                        s.copy(out=rb[:, hs, :], in_=PSB[6][64 * hs:64 * hs + 64, :])\
                            .then_inc(sS.h, 1)
                    P.em("scalar", _)
                    rb_ok = (sS, sS.inc())
                    BANKT[6] = rb_ok

                    def _(v, blk=blk, cp=cp, hs=hs, obk=obk, w=[rb_ok]):
                        _w(v, w)
                        v.tensor_mul(o_sb[64 * hs:64 * hs + 64, cp, 512 * blk:512 * blk + 512],
                                     PSB[obk][0:64, :], rb[:, hs, :]).then_inc(sV.h, 1)
                    P.em("vector", _)
                    BANKT[obk] = (sV, sV.inc())

        barrier()

    if upto in ("C", "E", "DR", "DR1", "CW"):
        womm_hist = []
        for jc in range(HC):
            buf = jc % 2

            def _(g, jc=jc, buf=buf,
                  wfree=(womm_hist[jc - 2] if jc >= 2 else None)):
                if wfree is not None:
                    _w(g, [wfree])
                g.dma_start(
                    out=wpool[:, buf, 0:HC, :],
                    in_=woT.rearrange("(c p) j -> p c j", p=128)[:, :, 128 * jc:128 * jc + 128],
                ).then_inc(qWs[jc % 2].h, 16)
            P.em("gpsimd", _)
            qWs[jc % 2].inc(16)
            wo_ld = qWs[jc % 2].t()

            for th2 in range(2):
                bk_ = 2 * buf + th2

                def _(t, jc=jc, buf=buf, th2=th2, bk_=bk_, w=[wo_ld, BANKT[bk_]]):
                    _w(t, w)
                    for hc in range(HC):
                        mm = t.matmul(PSB[bk_][:], wpool[:, buf, hc, :],
                                      o_sb[:, hc, 512 * th2:512 * th2 + 512],
                                      start=(hc == 0), stop=(hc == HC - 1))
                    mm.then_inc(sT.h, 1)
                P.em("tensor", _)
                ymm = (sT, sT.inc())
                if th2 == 1:
                    womm_hist.append(ymm)

                def _(s, jc=jc, th2=th2, bk_=bk_, w=[ymm]):
                    _w(s, w)
                    s.activation(out=yatt[:, jc, 512 * th2:512 * th2 + 512],
                                 in_=PSB[bk_][:], func=AF.Identity,
                                 bias=c_bo[:, jc:jc + 1], scale=1.0).then_inc(sS.h, 1)
                P.em("scalar", _)
                BANKT[bk_] = (sS, sS.inc())
        yatt_ok = sS.t()

        if debug and upto in ("DR1", "CW"):
            yatt_cap = at("yatt_cap", [128, 16, 512], F32, 142336, 212224)
            dbg["yatt_cap"] = yatt_cap

            def _(v, w=[yatt_ok]):
                _w(v, w)
                for hc in range(HC):
                    v.tensor_copy(yatt_cap[:, hc, :], yatt[:, hc, 0:512])
                v.nop().then_inc(sV.h, 1)
            P.em("vector", _)
            sV.inc()

        x2_spill = []
        for th in (range(2) if upto != "CW" else range(0)):
            bcA = emit_ln_stats(yatt[:, :, 512 * th:512 * th + 512], yatt_ok,
                                l_bc, (0, 1, 2, 3))
            if debug and upto == "DR1" and th == 0:
                lbc_cap = at("lbc_cap", [128, 2, 512], F32, 109568, 212224)
                lrb_cap = at("lrb_cap", [1, 2, 512], F32, 126016, 212224)

                lst_cap = at("lst_cap", [1, 4, 512], F32, 130112, 212224)

                def _(v, w=[bcA]):
                    _w(v, w)
                    v.tensor_copy(lbc_cap[:, 0, :], l_bc[:, 0, :])
                    v.tensor_copy(lbc_cap[:, 1, :], l_bc[:, 1, :])
                    v.tensor_copy(lrb_cap[:, 0, :], l_rb[0:1, 0, :])
                    v.tensor_copy(lrb_cap[:, 1, :], l_rb[0:1, 1, :])
                    for rr_ in range(4):
                        v.tensor_copy(lst_cap[:, rr_, :], l_stat[:, rr_, :])
                    v.nop().then_inc(sV.h, 1)
                P.em("vector", _)
                lbc_ok = (sV, sV.inc())
                dbg["lbc_ok"] = lbc_ok
                dbg["lbc_cap"] = lbc_cap
                dbg["lrb_cap"] = lrb_cap
                dbg["lst_cap"] = lst_cap
            xoc_tk = [None, None]
            sq2base = sS.val
            x2_ok = []
            for hc in range(HC):
                def _(sy, th=th, hc=hc, w=[yatt_ok, xoc_tk[hc % 2]]):
                    _w(sy, w)
                    sy.dma_start(
                        out=xoc[:, hc % 2, :],
                        in_=xTo.rearrange("(c p) t -> p c t", p=128)
                            [:, hc, 512 * th:512 * th + 512],
                    ).then_inc(qXC[hc % 2].h, 16)
                P.em("sync", _)
                qXC[hc % 2].inc(16)
                xoc_ld = qXC[hc % 2].t()

                def _(v, th=th, hc=hc, w=[bcA, xoc_ld]):
                    _w(v, w)
                    sl = yatt[:, hc, 512 * th:512 * th + 512]
                    v.tensor_mul(l_nrm[:], sl, l_bc[:, 0, :])
                    v.tensor_sub(l_nrm[:], l_nrm[:], l_bc[:, 1, :])
                    v.tensor_add(sl, l_nrm[:], xoc[:, hc % 2, :]).then_inc(sV.h, 1)
                P.em("vector", _)
                x2c_ok = (sV, sV.inc())
                xoc_tk[hc % 2] = x2c_ok
                x2_ok.append(x2c_ok)

                def _(sy, th=th, hc=hc, w=[x2c_ok]):
                    _w(sy, w)
                    sy.dma_start(
                        out=x2s.rearrange("(c p) t -> p c t", p=128)[:, hc, 512 * th:512 * th + 512],
                        in_=yatt[:, hc, 512 * th:512 * th + 512]).then_inc(qSp.h, 16)
                P.em("sync", _)
                qSp.inc(16)
                x2_spill.append(qSp.t())

                def _(s, th=th, hc=hc, w=[x2c_ok], tb=sT.val):
                    _w(s, w)
                    if hc >= 1:
                        s.wait_ge(sT.h, tb)
                    s.activation(out=l_sq[:, hc % 2, :],
                                 in_=yatt[:, hc, 512 * th:512 * th + 512],
                                 func=AF.Square).then_inc(sS.h, 1)
                P.em("scalar", _)
                sq_ok = (sS, sS.inc())

                def _(v, th=th, hc=hc, w=[x2c_ok], tb=sT.val):
                    _w(v, w)
                    if hc >= 1:
                        v.wait_ge(sT.h, tb)
                    v.tensor_copy(l_xb[:, hc % 2, :],
                                  yatt[:, hc, 512 * th:512 * th + 512]).then_inc(sV.h, 1)
                P.em("vector", _)
                xb_ok = (sV, sV.inc())

                def _(t, th=th, hc=hc, w=[xb_ok, sq_ok] + ([BANKT[4], BANKT[5]] if hc == 0 else [])):
                    _w(t, w)
                    t.matmul(PSB[4][0:1, :], c_onecb[:], l_xb[:, hc % 2, :],
                             start=(hc == 0), stop=(hc == HC - 1))
                    t.matmul(PSB[5][0:1, :], c_onecb[:], l_sq[:, hc % 2, :],
                             start=(hc == 0), stop=(hc == HC - 1)).then_inc(sT.h, 1)
                P.em("tensor", _)
                sT.inc()
            ln2stats_mm = sT.t()

            def _(v, w=[ln2stats_mm], base=sV.val):
                _w(v, w)
                k = base
                for fn in [
                    lambda: v.tensor_scalar_mul(l_stat[:, 0, :], PSB[4][0:1, :], 1.0 / D),
                    lambda: v.tensor_scalar_mul(l_stat[:, 1, :], PSB[5][0:1, :], 1.0 / D),
                    lambda: v.tensor_mul(l_stat[:, 2, :], l_stat[:, 0, :], l_stat[:, 0, :]),
                    lambda: v.tensor_sub(l_stat[:, 1, :], l_stat[:, 1, :], l_stat[:, 2, :]),
                    lambda: v.tensor_scalar_add(l_stat[:, 1, :], l_stat[:, 1, :], LN_EPS),
                ]:
                    fn().then_inc(sV.h, 1)
                    k += 1
                    v.wait_ge(sV.h, k)
            P.em("vector", _)
            sV.inc(5)
            var2_ok = sV.t()
            BANKT[4] = var2_ok
            BANKT[5] = var2_ok

            def _(s, w=[var2_ok]):
                _w(s, w)
                s.activation(out=l_stat[:, 2, :], in_=l_stat[:, 1, :], func=AF.Sqrt)\
                    .then_inc(sS.h, 1)
            P.em("scalar", _)
            sqrt2_ok = (sS, sS.inc())

            def _(v, w=[sqrt2_ok], base=sV.val):
                _w(v, w)
                v.reciprocal(l_stat[:, 2, :], l_stat[:, 2, :]).then_inc(sV.h, 1)
                v.wait_ge(sV.h, base + 1)
                v.tensor_mul(l_stat[:, 3, :], l_stat[:, 0, :], l_stat[:, 2, :])\
                    .then_inc(sV.h, 1)
            P.em("vector", _)
            sV.inc(2)
            rmul2_ok = sV.t()

            def _(s, w=[rmul2_ok]):
                _w(s, w)
                s.activation(out=l_rb[0:1, 0, :], in_=l_stat[:, 2, :], func=AF.Identity)
                s.activation(out=l_rb[0:1, 1, :], in_=l_stat[:, 3, :], func=AF.Identity)\
                    .then_inc(sS.h, 1)
            P.em("scalar", _)
            rstd2_ok = (sS, sS.inc())

            def _(t, w=[rstd2_ok, BANKT[2], BANKT[3]]):
                _w(t, w)
                t.matmul(PSB[2][:], c_e0[:, :], l_rb[:, 0, :], start=True, stop=True)
                t.matmul(PSB[3][:], c_e0[:, :], l_rb[:, 1, :], start=True, stop=True)\
                    .then_inc(sT.h, 1)
            P.em("tensor", _)
            bc2_mm = (sT, sT.inc())

            def _(s, w=[bc2_mm]):
                _w(s, w)
                s.copy(out=l_bc2[:, 0, :], in_=PSB[2][:])
                s.copy(out=l_bc2[:, 1, :], in_=PSB[3][:]).then_inc(sS.h, 1)
            P.em("scalar", _)
            bc2_ok = (sS, sS.inc())
            BANKT[2] = bc2_ok
            BANKT[3] = bc2_ok

            def _(v, th=th, w=[bc2_ok]):
                _w(v, w)
                for hc in range(HC):
                    v.tensor_mul(l_nrm[:], yatt[:, hc, 512 * th:512 * th + 512], l_bc2[:, 0, :])
                    op = v.tensor_sub(ln2_sb[:, hc, 512 * th:512 * th + 512], l_nrm[:], l_bc2[:, 1, :])
                op.then_inc(sV.h, 1)
            P.em("vector", _)
            sV.inc()
        ln2_ok = sV.t()

        barrier()
    if upto == "CW":
        pass

    if upto in ("E", "DR", "DR1"):
        def _(v, w=[ln2_ok]):
            _w(v, w)
            v.memset(combT[:, :], 0.0).then_inc(sV.h, 1)
        P.em("vector", _)
        combT_z = (sV, sV.inc())

        comb_ok = []
        prev_ex = None
        prev_tr = None
        RB = PSB[0] if upto == "DR1" else PSB[6]
        for tsub in range(8):
            def _(t, tsub=tsub, w=[ln2_ok, gate_ld, BANKT[6]]):
                _w(t, w)
                for hc in range(HC):
                    mm = t.matmul(RB[:, 0:NE], ln2_sb[:, hc, 128 * tsub:128 * tsub + 128],
                                  c_gateT[:, hc, :], start=(hc == 0), stop=(hc == HC - 1))
                mm.then_inc(sT.h, 1)
            P.em("tensor", _)
            lg_mm = (sT, sT.inc())

            if upto == "DR1":
                def _(v, w=[lg_mm, prev_ex]):
                    _w(v, w)
                    v.tensor_copy(r_sc[:, 0, :], RB[:, 0:NE]).then_inc(sV.h, 1)
                P.em("vector", _)
                nm_ok = (sV, sV.inc())
            else:
                def _(v, w=[lg_mm, prev_ex]):
                    _w(v, w)
                    v.reduce_max(r_m[:, 0:1], RB[:, 0:NE], axis=AX.X, negate=True)\
                        .then_inc(sV.h, 1)
                P.em("vector", _)
                nm_ok = (sV, sV.inc())

            def _(s, w=[nm_ok], skip=(upto == "DR1")):
                _w(s, w)
                if skip:
                    s.nop().then_inc(sS.h, 1)
                else:
                    s.activation(out=r_sc[:, 0, :], in_=RB[:, 0:NE], func=AF.Exp,
                                 bias=r_m[:, 0:1], scale=1.0).then_inc(sS.h, 1)
            P.em("scalar", _)
            ex_ok = (sS, sS.inc())
            prev_ex = ex_ok
            BANKT[6] = ex_ok

            def _(v, tsub=tsub, w=[ex_ok], skip=(upto == "DR1"), base=sV.val):
                _w(v, w)
                ops = [
                    lambda: v.nop() if skip else v.reduce_sum(r_m[:, 1:2], r_sc[:, 0, :], axis=AX.X),
                    lambda: v.nop() if skip else v.reciprocal(r_m[:, 1:2], r_m[:, 1:2]),
                    lambda: v.nop() if skip else v.tensor_scalar_mul(r_sc[:, 0, :], r_sc[:, 0, :], r_m[:, 1:2]),
                    lambda: v.nop() if skip else v.reduce_max(r_m[:, 2:3], r_sc[:, 0, :], axis=AX.X),
                    lambda: v.nop() if skip else v.tensor_single_scalar(r_sc[:, 1, :], r_sc[:, 0, :], r_m[:, 2:3], op=ALU.is_ge),
                    lambda: v.nop() if skip else v.tensor_scalar_mul(r_sc[:, 1, :], r_sc[:, 1, :], 1.0e30),
                    lambda: v.nop() if skip else v.tensor_sub(r_sc[:, 1, :], r_sc[:, 0, :], r_sc[:, 1, :]),
                    lambda: v.nop() if skip else v.reduce_max(r_m[:, 3:4], r_sc[:, 1, :], axis=AX.X),
                    lambda: v.nop() if skip else v.tensor_single_scalar(r_sc[:, 1, :], r_sc[:, 0, :], r_m[:, 3:4], op=ALU.is_ge),
                    lambda: v.nop() if skip else v.tensor_mul(r_sc[:, 0, :], r_sc[:, 0, :], r_sc[:, 1, :]),
                ]
                k = base
                for fn in ops:
                    fn().then_inc(sV.h, 1)
                    k += 1
                    v.wait_ge(sV.h, k)
            P.em("vector", _)
            sV.inc(10)
            cmb_ok = sV.t()

            def _(v, tsub=tsub, w=[cmb_ok, prev_tr], dodiag=(debug and upto == "DR1")):
                _w(v, w)
                v.tensor_copy(r_cb[:, :], r_sc[:, 0, :]).then_inc(sV.h, 1)
                if dodiag:
                    v.tensor_copy(st1.rearrange("p (a b) -> p a b", b=16)[:, tsub, :],
                                  r_sc[:, 0, :])
                    v.tensor_copy(st2.rearrange("p (a b) -> p a b", b=16)[:, tsub, :],
                                  r_cb[:, :])
            P.em("vector", _)
            cbb_ok = (sV, sV.inc())

            def _(t, tsub=tsub, w=[cbb_ok, BANKT[7]]):
                _w(t, w)
                t.matmul(PSB[7][0:NE, 0:128], r_cb[:, :], c_idb[:],
                         start=True, stop=True).then_inc(sT.h, 1)
            P.em("tensor", _)
            tr_mm = (sT, sT.inc())
            prev_tr = tr_mm

            def _(v, tsub=tsub, w=[tr_mm]):
                _w(v, w)
                v.tensor_copy(combT[0:NE, 128 * tsub:128 * tsub + 128], PSB[7][0:NE, 0:128])\
                    .then_inc(sV.h, 1)
            P.em("vector", _)
            ct_ok = (sV, sV.inc())
            BANKT[7] = ct_ok
            comb_ok.append(ct_ok)

        gumm_hist = []
        dnmm_hist = []
        st1_guard = [None, None]
        ngu = 0
        ndn = 0
        h_ok = {}
        h_users = {}
        cb_ok = {}
        cb_users = {}
        dn_evac = {}

        def emit_gu(u):
            nonlocal ngu
            is_sh = u < 2
            e = u - 2
            hbuf = h_bu[u % 2]
            hck = []
            for jc in range(11):
                bufg = 2 * (ngu % 2)
                bufu = bufg + 1

                def _(g, u=u, e=e, jc=jc, is_sh=is_sh, bufg=bufg, bufu=bufu, ngu=ngu,
                      wfree=(gumm_hist[ngu - 2] if ngu >= 2 else None)):
                    if wfree is not None:
                        _w(g, [wfree])
                    if is_sh:
                        gsrc = sgu.rearrange("(c p) j -> p c j", p=128)
                        g0 = 1408 * u + 128 * jc
                        u0 = SI + 1408 * u + 128 * jc
                    else:
                        gsrc = wgu[e].rearrange("(c p) j -> p c j", p=128)
                        g0 = 128 * jc
                        u0 = I_FF + 128 * jc
                    for part in range(4):
                        g.dma_start(out=wpool[:, bufg, 4 * part:4 * part + 4, :],
                                    in_=gsrc[:, 4 * part:4 * part + 4, g0:g0 + 128]).then_inc(qWs[ngu % 2].h, 16)
                    for part in range(4):
                        g.dma_start(out=wpool[:, bufu, 4 * part:4 * part + 4, :],
                                    in_=gsrc[:, 4 * part:4 * part + 4, u0:u0 + 128]).then_inc(qWs[ngu % 2].h, 16)
                P.em("gpsimd", _)
                qWs[ngu % 2].inc(8 * 16)
                guw_ld = qWs[ngu % 2].t()

                for th2 in range(2):
                    bkg = (2 * jc + th2) % 2
                    bku = 2 + (2 * jc + th2) % 2

                    def _(t, jc=jc, th2=th2, bufg=bufg, bufu=bufu, bkg=bkg, bku=bku,
                          w=[guw_ld, ln2_ok, BANKT[bkg], BANKT[bku]]):
                        _w(t, w)
                        for hc in range(HC):
                            t.matmul(PSB[bkg][:], wpool[:, bufg, hc, :],
                                     ln2_sb[:, hc, 512 * th2:512 * th2 + 512],
                                     start=(hc == 0), stop=(hc == HC - 1))
                        for hc in range(HC):
                            mm = t.matmul(PSB[bku][:], wpool[:, bufu, hc, :],
                                          ln2_sb[:, hc, 512 * th2:512 * th2 + 512],
                                          start=(hc == 0), stop=(hc == HC - 1))
                        mm.then_inc(sT.h, 1)
                    P.em("tensor", _)
                    gu_mm = (sT, sT.inc())
                    if th2 == 1:
                        gumm_hist.append(gu_mm)

                    def _(s, th2=th2, bkg=bkg, w=[gu_mm, st1_guard[th2]]):
                        _w(s, w)
                        s.activation(out=st1[:, 512 * th2:512 * th2 + 512],
                                     in_=PSB[bkg][:], func=AF.Silu).then_inc(sS.h, 1)
                    P.em("scalar", _)
                    silu_ok = (sS, sS.inc())
                    BANKT[bkg] = silu_ok

                    def _(v, u=u, jc=jc, th2=th2, bku=bku, is_sh=is_sh, hbuf=hbuf,
                          w=[silu_ok, h_users.get(u - 2), cb_ok.get(u)]):
                        _w(v, w)
                        sl = slice(512 * th2, 512 * th2 + 512)
                        v.tensor_mul(st2[:, sl], st1[:, sl], PSB[bku][:, :])
                        if is_sh:
                            op = v.tensor_copy(hbuf[:, jc, sl], st2[:, sl])
                        else:
                            op = v.tensor_mul(hbuf[:, jc, sl], st2[:, sl], cb[:, u % 2, sl])
                        op.then_inc(sV.h, 1)
                    P.em("vector", _)
                    hc_ok = (sV, sV.inc())
                    st1_guard[th2] = hc_ok
                    BANKT[bku] = hc_ok
                    if not is_sh:
                        cb_users[u] = hc_ok
                    hck.append(hc_ok)
                ngu += 1
            h_ok[u] = hck[-1]

        def emit_dn(u):
            nonlocal ndn
            is_sh = u < 2
            e = u - 2
            hbuf = h_bu[u % 2]
            last_mm = None
            for hcs in range(8):
                bufd = ndn % 2

                def _(g, u=u, e=e, hcs=hcs, is_sh=is_sh, bufd=bufd, ndn=ndn,
                      wfree=(dnmm_hist[ndn - 2] if ndn >= 2 else None)):
                    if wfree is not None:
                        _w(g, [wfree])
                    if is_sh:
                        dsrc = sdn.rearrange("(c p) h -> p c h", p=128)[:, 11 * u:11 * u + 11, :]
                    else:
                        dsrc = wdn[e].rearrange("(c p) h -> p c h", p=128)
                    for part in range(2):
                        g.dma_start(
                            out=dpool[:, bufd, :, 128 * part:128 * part + 128],
                            in_=dsrc[:, :, 256 * hcs + 128 * part:256 * hcs + 128 * part + 128],
                        ).then_inc(qD[ndn % 2].h, 16)
                P.em("gpsimd", _)
                qD[ndn % 2].inc(32)
                dnw_ld = qD[ndn % 2].t()

                for part in range(2):
                    hc = 2 * hcs + part
                    for th2 in range(2):
                        bkd = 4 + (2 * hc + th2) % 2

                        def _(t, u=u, hc=hc, part=part, th2=th2, bufd=bufd, bkd=bkd,
                              hbuf=hbuf, w=[dnw_ld, h_ok[u], BANKT[bkd]]):
                            _w(t, w)
                            for ic in range(11):
                                mm = t.matmul(PSB[bkd][:], dpool[:, bufd, ic, 128 * part:128 * part + 128],
                                              hbuf[:, ic, 512 * th2:512 * th2 + 512],
                                              start=(ic == 0), stop=(ic == 10))
                            mm.then_inc(sT.h, 1)
                        P.em("tensor", _)
                        dn_mm = (sT, sT.inc())
                        last_mm = dn_mm

                        def _(v, u=u, hc=hc, th2=th2, bkd=bkd,
                              w=[dn_mm] + ([x2_spill[-1]] if u == 0 else []),
                              first=(u == 0)):
                            _w(v, w)
                            sl = slice(512 * th2, 512 * th2 + 512)
                            if first:
                                op = v.tensor_copy(y_sb[:, hc, sl], PSB[bkd][:])
                            else:
                                op = v.tensor_add(y_sb[:, hc, sl], PSB[bkd][:], y_sb[:, hc, sl])
                            op.then_inc(sV.h, 1)
                        P.em("vector", _)
                        BANKT[bkd] = (sV, sV.inc())
                dnmm_hist.append(last_mm)
                ndn += 1
            h_users[u] = last_mm
            dn_evac[u] = sV.t()

        def emit_cb(u):
            e = u - 2
            for th2 in range(2):
                def _(t, e=e, th2=th2, w=[comb_ok[-1], BANKT[6], cb_users.get(u - 2),
                                           combT_z]):
                    _w(t, w)
                    t.matmul(PSB[6][:], c_oneh[:, 128 * e:128 * e + 128],
                             combT[:, 512 * th2:512 * th2 + 512],
                             start=True, stop=True).then_inc(sT.h, 1)
                P.em("tensor", _)
                cbmm = (sT, sT.inc())

                def _(s, u=u, th2=th2, w=[cbmm]):
                    _w(s, w)
                    s.copy(out=cb[:, u % 2, 512 * th2:512 * th2 + 512], in_=PSB[6][:])\
                        .then_inc(sS.h, 1)
                P.em("scalar", _)
                cbc = (sS, sS.inc())
                BANKT[6] = cbc
            cb_ok[u] = cbc

        def emit_gu2(u):
            if u >= 2:
                emit_cb(u)
            emit_gu(u)

        UNITS = range(18) if upto == "E" else range(0)
        for u in UNITS:
            emit_gu2(u)
            if u >= 1:
                emit_dn(u - 1)
        if upto == "E":
            emit_dn(17)
        y_ok = sV.t()

        barrier()

    if upto == "E":
        out_dma = None
        for th in range(2):
            bcF = emit_ln_stats(y_sb[:, :, 512 * th:512 * th + 512], y_ok,
                                l_bc, (0, 1, 2, 3), src_is_bf16=True)
            xo2_tk = [None, None]
            ost_tk = [None, None]
            for hc in range(HC):
                def _(sy, th=th, hc=hc, w=[x2_spill[-1], xo2_tk[hc % 2]]):
                    _w(sy, w)
                    sy.dma_start(
                        out=xo2[:, hc % 2, :],
                        in_=x2s.rearrange("(c p) t -> p c t", p=128)[:, hc, 512 * th:512 * th + 512],
                    ).then_inc(qR[hc % 2].h, 16)
                P.em("sync", _)
                qR[hc % 2].inc(16)
                x2_ld = qR[hc % 2].t()

                def _(v, th=th, hc=hc, w=[bcF, x2_ld, ost_tk[hc % 2]]):
                    _w(v, w)
                    v.tensor_mul(l_nrm[:], y_sb[:, hc, 512 * th:512 * th + 512], l_bc[:, 0, :])
                    v.tensor_sub(l_nrm[:], l_nrm[:], l_bc[:, 1, :])
                    v.tensor_add(ost[:, hc % 2, :], l_nrm[:], xo2[:, hc % 2, :]).then_inc(sV.h, 1)
                P.em("vector", _)
                o_ok = (sV, sV.inc())
                xo2_tk[hc % 2] = o_ok

                def _(sy, th=th, hc=hc, w=[o_ok]):
                    _w(sy, w)
                    sy.dma_start(
                        out=out.rearrange("(c p) t -> p c t", p=128)[:, hc, 512 * th:512 * th + 512],
                        in_=ost[:, hc % 2, :]).then_inc(qO[hc % 2].h, 16)
                P.em("sync", _)
                qO[hc % 2].inc(16)
                out_dma = qO[hc % 2].t()
                ost_tk[hc % 2] = out_dma

    if upto != "E":
        def _(v, w=[sT.t(), sV.t(), sS.t()]):
            _w(v, w)
            v.memset(l_nrm[:], 0.0).then_inc(sV.h, 1)
        P.em("vector", _)
        z_ok = (sV, sV.inc())

        def _(sy, w=[z_ok]):
            _w(sy, w)
            for cc in range(HC):
                for th in range(2):
                    sy.dma_start(
                        out=out.rearrange("(c p) t -> p c t", p=128)[:, cc, 512 * th:512 * th + 512],
                        in_=l_nrm[:]).then_inc(qO[0].h, 16)
        P.em("sync", _)
        qO[0].inc(16 * HC * 2)
        out_dma = qO[0].t()

    if debug:
        have_kv = upto in ("A", "A2", "B")
        have_q = upto in ("A2", "B")
        have_o = upto in ("B", "CW")
        ndbg = (9 if have_kv else 0) + (1 if have_q else 0) + (1 if have_o else 0) \
            + (4 if upto in ("E", "DR", "DR1") else 0) + (6 if upto == "DR1" else 0) \
            + (1 if upto == "CW" else 0)

        def _(g, w=[out_dma, sT.t(), sV.t(), sS.t()]):
            _w(g, w)
            if have_kv:
                g.dma_start(out=dbg["d_k"].rearrange("(c p) t -> p c t", p=128),
                            in_=k_sb[:, :, :]).then_inc(qDBG.h, 16)
                for h8 in range(8):
                    g.dma_start(
                        out=dbg["d_v"].rearrange("(c p) d -> p c d", p=128)[:, :, 64 * h8:64 * h8 + 64],
                        in_=v_sb[:, :, 65 * h8:65 * h8 + 64],
                    ).then_inc(qDBG.h, 16)
            if have_q:
                g.dma_start(out=dbg["d_q"].rearrange("(c p) t -> p c t", p=128),
                            in_=q_sb[:, :, :]).then_inc(qDBG.h, 16)
            if have_o:
                g.dma_start(out=dbg["d_o"].rearrange("(c p) t -> p c t", p=128),
                            in_=o_sb[:, :, :]).then_inc(qDBG.h, 16)
            if upto in ("DR1", "CW"):
                if "yatt_cap" in dbg:
                    g.dma_start(out=dbg["d_yatt"].rearrange("(c p) t -> p c t", p=128),
                                in_=dbg["yatt_cap"][:, :, :]).then_inc(qDBG.h, 16)
                if "lbc_cap" in dbg:
                    g.dma_start(out=dbg["d_lbc"][:, :, :], in_=dbg["lbc_cap"][:, :, :]).then_inc(qDBG.h, 16)
                    g.dma_start(out=dbg["d_lrb"][:, :, :], in_=dbg["lrb_cap"][:, :, :]).then_inc(qDBG.h, 16)
                    g.dma_start(out=dbg["d_lst"][:, :, :], in_=dbg["lst_cap"][:, :, :]).then_inc(qDBG.h, 16)
                g.dma_start(out=dbg["d_rsc"][:, :, :],
                            in_=st1.rearrange("p (a b) -> p a b", b=16)[:, 0:8, :]).then_inc(qDBG.h, 16)
                g.dma_start(out=dbg["d_rcb"][:, :, :],
                            in_=st2.rearrange("p (a b) -> p a b", b=16)[:, 0:8, :]).then_inc(qDBG.h, 16)
            if upto in ("E", "DR", "DR1"):
                g.dma_start(out=dbg["d_x2"][:, :], in_=x2s[:, :]).then_inc(qDBG.h, 16)
                g.dma_start(out=dbg["d_ln2"].rearrange("(c p) t -> p c t", p=128),
                            in_=ln2_sb[:, :, :]).then_inc(qDBG.h, 16)
                g.dma_start(out=dbg["d_comb"][:, :], in_=combT[0:NE, :]).then_inc(qDBG.h, 16)
                g.dma_start(out=dbg["d_y"].rearrange("(c p) t -> p c t", p=128),
                            in_=y_sb[:, :, :]).then_inc(qDBG.h, 16)
        P.em("gpsimd", _)
        qDBG.inc(16 * ndbg)

    _allq = [qC, qG, qKV, qM, qQt, qXA, qKt[0], qKt[1], qKS[0], qKS[1],
             qQS[0], qQS[1], qWs[0], qWs[1], qD[0], qD[1], qXC[0], qXC[1],
             qR[0], qR[1], qO[0], qO[1], qSp, qDBG]

    def _(sy, finals=[(q, q.val) for q in _allq if q.val > 0]):
        _w(sy, finals)
    P.em("sync", _)

    P.flush()
    es.close()
    return nc



def _rope_tables(positions):
    j = np.arange(32, dtype=np.float64)
    theta = 10000.0 ** (-2.0 * j / 64.0)
    pos = (positions.astype(np.float64) + 1.0)[None, :]
    ang = theta[:, None] * pos
    ang64 = np.concatenate([ang, ang], axis=0)
    cos64 = np.cos(ang64)
    sin64 = np.sin(ang64)
    sgn = np.where(np.arange(64) < 32, -1.0, 1.0)[:, None]
    sinp = sin64 * sgn
    cos128 = np.concatenate([cos64, cos64], axis=0).astype(np.float32)
    sin128 = np.concatenate([sinp, sinp], axis=0).astype(np.float32)
    return np.ascontiguousarray(cos128), np.ascontiguousarray(sin128)


def _masks_for_core(h):
    m = np.zeros((24, 128, 512), np.float32)
    for blk in range(2):
        g = 1024 * h + 512 * blk
        for tk in range(BLK_NSLOT[blk]):
            mi = _mask_idx(blk, tk)
            if mi is None:
                continue
            ii = 128 * tk + np.arange(128)[:, None]
            jj = g + np.arange(512)[None, :]
            m[mi] = (ii <= jj).astype(np.float32)
    return m.astype(ml_dtypes.bfloat16)


def _prep_inputs(inputs):
    x = np.asarray(inputs["L0_attn_x"], np.float32)
    perm = PERM_M
    shared = dict(
        wqT=np.ascontiguousarray(np.asarray(inputs["L0_attn_Wq"], np.float32).T[:, perm]),
        bq2=np.ascontiguousarray(np.asarray(inputs["L0_attn_bq"], np.float32)[perm].reshape(HC, 128).T),
        wkT=np.ascontiguousarray(np.asarray(inputs["L0_attn_Wk"], np.float32).T),
        bk2=np.ascontiguousarray(np.asarray(inputs["L0_attn_bk"], np.float32).reshape(4, 128).T),
        wvT=np.ascontiguousarray(np.asarray(inputs["L0_attn_Wv"], np.float32).T),
        bv=np.asarray(inputs["L0_attn_bv"], np.float32),
        woT=np.ascontiguousarray(np.asarray(inputs["L0_attn_Wo"], np.float32).T[perm, :]),
        bo2=np.ascontiguousarray(np.asarray(inputs["L0_attn_bo"], np.float32).reshape(HC, 128).T),
        gateT=np.ascontiguousarray(np.asarray(inputs["L0_ffn_gate_w"], np.float32).T),
        wgu=np.ascontiguousarray(np.asarray(inputs["L0_ffn_w_gate_up"], np.float32)),
        wdn=np.ascontiguousarray(np.asarray(inputs["L0_ffn_w_down"], np.float32)),
        sgu=np.ascontiguousarray(np.asarray(inputs["L0_ffn_shared_gate_up"], np.float32)),
        sdn=np.ascontiguousarray(np.asarray(inputs["L0_ffn_shared_down"], np.float32)),
        ones1=np.ones((1, 128), np.float32),
        ones1b=np.ones((1, 128), ml_dtypes.bfloat16),
        onesr=np.ones((1, 512), ml_dtypes.bfloat16),
        onec1=np.ones((128, 1), np.float32),
        onec1b=np.ones((128, 1), ml_dtypes.bfloat16),
        identf=np.eye(128, dtype=np.float32),
    )
    onehot = np.zeros((128, 2048), np.float32)
    for e in range(NE):
        onehot[e, 128 * e:128 * e + 128] = 1.0
    shared["onehot"] = onehot.astype(ml_dtypes.bfloat16)
    shared["identb"] = np.eye(128, dtype=ml_dtypes.bfloat16)
    e0m = np.zeros((128, 128), np.float32)
    e0m[0, :] = 1.0
    shared["e0m"] = e0m.astype(ml_dtypes.bfloat16)
    shared["cosk"], shared["sink"] = _rope_tables(np.arange(SEQ))

    in_maps = []
    xT_b = [np.ascontiguousarray(x[b].T) for b in range(BATCH)]
    for c in range(NCORES):
        b, h = c // 2, c % 2
        cq, sq = _rope_tables(np.arange(1024 * h, 1024 * h + 1024))
        m = dict(shared)
        m["xT"] = xT_b[b]
        m["xTo"] = np.ascontiguousarray(xT_b[b][:, 1024 * h:1024 * h + 1024])
        m["cosq"] = cq
        m["sinq"] = sq
        m["masks"] = _masks_for_core(h)
        in_maps.append(m)
    return in_maps


def kernel(**inputs):
    in_maps = _prep_inputs(inputs)
    nc = build_program(upto="E", debug=False)
    res = run_bass_kernel_spmd(nc, in_maps, core_ids=list(range(NCORES)))
    o = np.empty((BATCH, SEQ, D), np.float32)
    for c in range(NCORES):
        b, h = c // 2, c % 2
        o[b, 1024 * h:1024 * h + 1024, :] = res.results[c]["out"].T
    return o


# version 07
# speedup vs baseline: 1.0651x; 1.0651x over previous
import numpy as np
import ml_dtypes
from contextlib import ExitStack

import concourse.bass as bass
import concourse.mybir as mybir
from concourse.bass_utils import run_bass_kernel_spmd

F32 = mybir.dt.float32
BF16 = mybir.dt.bfloat16
U8 = mybir.dt.uint8
AF = mybir.ActivationFunctionType
ALU = mybir.AluOpType
AX = mybir.AxisListType

D = 2048
SEQ = 2048
BATCH = 4
NCORES = 8
TOWN = 1024
HC = 16
NHEAD = 32
NE = 16
I_FF = 1408
SI = 2816
EXP_SHIFT = 12.0
LN_EPS = 1e-5

EVENKV = [h for h in range(NHEAD) if (h // 4) % 2 == 0]
ODDKV = [h for h in range(NHEAD) if (h // 4) % 2 == 1]
PERM_M = np.zeros(D, np.int64)
for _c in range(16):
    for _s, _h in enumerate((EVENKV[_c], ODDKV[_c])):
        PERM_M[128 * _c + 64 * _s: 128 * _c + 64 * _s + 64] = _h * 64 + np.arange(64)

BLK_NSLOT = [12, 16]


def _mask_idx(blk, tk):
    if blk == 0:
        return tk
    return 12 + (tk - 4) if tk >= 4 else None


ENGS = ("sync", "gpsimd", "tensor", "vector", "scalar")


class Sem:
    def __init__(self, handle):
        self.h = handle
        self.val = 0

    def inc(self, n=1):
        self.val += n
        return self.val

    def t(self):
        return (self, self.val)


class Prog:
    def __init__(self, nc):
        self.nc = nc
        self.streams = {k: [] for k in ENGS}

    def em(self, engine, fn):
        self.streams[engine].append(fn)

    def flush(self):
        with self.nc.Block() as B:
            for name in ENGS:
                fns = self.streams[name]
                getattr(B, name)(lambda eng, fns=fns: [f(eng) for f in fns])


def _w(eng, waits):
    seen = {}
    order = []
    for tk in waits:
        if tk is None:
            continue
        s, v = tk
        if s.h not in seen or seen[s.h] < v:
            if s.h not in seen:
                order.append(s.h)
            seen[s.h] = max(seen.get(s.h, 0), v)
    for h in order:
        eng.wait_ge(h, seen[h])


def build_program(upto="E", debug=False):
    nc = bass.Bass("TRN2", target_bir_lowering=False)
    P = Prog(nc)
    es = ExitStack()

    def din(name, shape, dtype=F32):
        return nc.dram_tensor(name, list(shape), dtype, kind="ExternalInput").ap()

    xT = din("xT", [D, SEQ])
    xTo = din("xTo", [D, TOWN])
    wqT = din("wqT", [D, D])
    bq2 = din("bq2", [128, HC])
    wkT = din("wkT", [D, 512])
    bk2 = din("bk2", [128, 4])
    wvT = din("wvT", [D, 512])
    bv = din("bv", [512])
    woT = din("woT", [D, D])
    bo2 = din("bo2", [128, HC])
    gateT = din("gateT", [D, NE])
    wgu = din("wgu", [NE, D, 2 * I_FF])
    wdn = din("wdn", [NE, I_FF, D])
    sgu = din("sgu", [D, 2 * SI])
    sdn = din("sdn", [SI, D])
    cosk = din("cosk", [128, SEQ])
    sink = din("sink", [128, SEQ])
    cosq = din("cosq", [128, TOWN])
    sinq = din("sinq", [128, TOWN])
    masksI = din("masks", [24, 128, 512], BF16)
    ones1 = din("ones1", [1, 128])
    ones1b = din("ones1b", [1, 128], BF16)
    onesr = din("onesr", [1, 512], BF16)
    e0m = din("e0m", [128, 128], BF16)
    onec1 = din("onec1", [128, 1])
    onec1b = din("onec1b", [128, 1], BF16)
    identf = din("identf", [128, 128])
    identb = din("identb", [128, 128], BF16)
    onehotI = din("onehot", [128, 2048], BF16)
    out = nc.dram_tensor("out", [D, TOWN], F32, kind="ExternalOutput").ap()
    x2s = nc.dram_tensor("x2s", [D, TOWN], F32, kind="Internal").ap()

    dbg = {}
    if debug:
        dbg["d_rsc"] = nc.dram_tensor("d_rsc", [128, 8, 16], F32, kind="ExternalOutput").ap()
        dbg["d_rcb"] = nc.dram_tensor("d_rcb", [128, 8, 16], F32, kind="ExternalOutput").ap()
        dbg["d_lbc"] = nc.dram_tensor("d_lbc", [128, 2, 512], F32, kind="ExternalOutput").ap()
        dbg["d_lrb"] = nc.dram_tensor("d_lrb", [1, 2, 512], F32, kind="ExternalOutput").ap()
        dbg["d_lst"] = nc.dram_tensor("d_lst", [1, 4, 512], F32, kind="ExternalOutput").ap()
        dbg["d_yatt"] = nc.dram_tensor("d_yatt", [D, 512], F32, kind="ExternalOutput").ap()
        for name, shape in [("d_k", [512, SEQ]), ("d_v", [SEQ, 512]),
                            ("d_q", [D, TOWN]), ("d_o", [D, TOWN]),
                            ("d_x2", [D, TOWN]), ("d_ln2", [D, TOWN]),
                            ("d_comb", [NE, TOWN]), ("d_y", [D, TOWN])]:
            dbg[name] = nc.dram_tensor(name, shape, F32, kind="ExternalOutput").ap()

    ARENA = 212224
    arena = es.enter_context(nc.sbuf_tensor("arena", [128, ARENA], U8))
    base = nc.lookup_mloc(arena).addr

    def at(name, shape, dtype, off, end):
        nbytes = int(np.prod(shape[1:])) * mybir.dt.size(dtype)
        assert off + nbytes <= end, f"{name}: {off}+{nbytes} > {end}"
        h = nc.alloc_sbuf_tensor_at(name, list(shape), dtype, offset=base + off)
        return h.ap()

    c_ones = at("c_ones", [1, 128], F32, 0, 16384)
    c_onesb = at("c_onesb", [1, 128], BF16, 512, 16384)
    c_id = at("c_id", [128, 128], F32, 768, 16384)
    c_idb = at("c_idb", [128, 128], BF16, 14464, 16384)
    c_oneh = at("c_oneh", [128, 2048], BF16, 1280, 16384)
    c_gateT = at("c_gateT", [128, HC, NE], BF16, 9472, 16384)
    c_bq = at("c_bq", [128, HC], F32, 9984, 16384)
    c_bk = at("c_bk", [128, 4], F32, 10048, 16384)
    c_bo = at("c_bo", [128, HC], F32, 10080, 16384)
    c_bvb = at("c_bvb", [128, 512], F32, 10144, 16384)
    c_negC = at("c_negC", [128, 1], F32, 12192, 16384)
    c_onec = at("c_onec", [128, 1], F32, 14336, 16384)
    c_onecb = at("c_onecb", [128, 1], BF16, 14400, 16384)
    c_onesr = at("c_onesr", [1, 512], BF16, 13280, 16384)
    c_e0 = at("c_e0", [128, 128], BF16, 15744, 16384)
    bv_row = at("bv_row", [1, 512], F32, 12224, 16384)
    bv_rowb = at("bv_rowb", [128, 512], BF16, 14720, 16384)
    l_rb = at("l_rb", [128, 2, 512], BF16, 38912, 43008)
    l_sq = at("l_sq", [128, 2, 512], BF16, 16384, 43008)
    l_xb = at("l_xb", [128, 2, 512], BF16, 18432, 43008)
    l_nrm = at("l_nrm", [128, 512], F32, 20480, 43008)
    l_stat = at("l_stat", [1, 4, 512], F32, 22528, 43008)
    l_bc = at("l_bc", [128, 2, 512], F32, 30720, 43008)
    l_bc2 = at("l_bc2", [128, 2, 512], F32, 34816, 43008)
    wpool = at("wpool", [128, 4, 16, 128], BF16, 43008, 76800)
    dpool = at("dpool", [128, 2, 11, 256], BF16, 59392, 76800)

    AB = 76800

    k_sb = at("k_sb", [128, 4, SEQ], BF16, AB, 212224)
    v_sb = at("v_sb", [128, HC, 8 * 65], BF16, 93184, 212224)
    xf = at("xf", [128, HC, 512], F32, 109824, 212224)
    lf = at("lf", [128, HC, 512], BF16, 142592, 212224)
    wk_sb = at("wk_sb", [128, HC, 512], BF16, 158976, 212224)
    wv_sb = at("wv_sb", [128, HC, 512], BF16, 175360, 212224)
    ckc = at("ckc", [128, 2, 512], F32, 191744, 212224)
    skc = at("skc", [128, 2, 512], F32, 195840, 212224)
    kraw = at("kraw", [128, 2, 512], F32, 199936, 212224)
    kswp = at("kswp", [128, 2, 512], F32, 204032, 212224)
    kt1 = at("kt1", [128, 512], F32, 208128, 212224)
    kt2 = at("kt2", [128, 512], F32, 210176, 212224)
    q_sb = at("q_sb", [128, HC, TOWN], BF16, 109824, 212224)
    ln1o = at("ln1o", [128, HC, 512], BF16, 142592, 212224)
    xo = at("xo", [128, HC, 512], F32, 158976, 212224)
    cq_sb = at("cq_sb", [128, TOWN], F32, 191744, 212224)
    sq_sb = at("sq_sb", [128, TOWN], F32, 195840, 212224)
    qraw = at("qraw", [128, 2, 512], F32, 199936, 212224)
    qswp = at("qswp", [128, 2, 512], F32, 204032, 212224)
    qt1 = at("qt1", [128, 512], F32, 208128, 212224)
    qt2 = at("qt2", [128, 512], F32, 210176, 212224)
    m_sb = at("m_sb", [128, 24, 512], BF16, 142592, 212224)
    pt = at("pt", [128, 8, 512], BF16, 167168, 212224)
    o_sb = at("o_sb", [128, HC, TOWN], BF16, 175360, 212224)
    rb = at("rb", [64, 2, 512], F32, 16384, 43008)
    rrow = at("rrow", [1, 2, 512], F32, 22528, 43008)
    rrowb = at("rrowb", [128, 2, 512], BF16, 26624, 43008)
    yatt = at("yatt", [128, HC, TOWN], F32, AB, 212224)
    ln2_sb = at("ln2_sb", [128, HC, TOWN], BF16, 142336, 212224)
    xoc = at("xoc", [128, 2, 512], F32, 208128, 212224)
    y_sb = at("y_sb", [128, HC, TOWN], BF16, AB, 212224)
    h_bu = [at("h_b0", [128, 11, TOWN], BF16, 109568, 212224),
            at("h_b1", [128, 11, TOWN], BF16, 175104, 212224)]
    st1 = at("st1", [128, TOWN], F32, 132096, 212224)
    st2 = at("st2", [128, TOWN], F32, 136192, 212224)
    r_sc = at("r_sc", [128, 2, NE], F32, 140288, 212224)
    r_m = at("r_m", [128, 4], F32, 140544, 212224)
    cb = at("cb", [128, 2, TOWN], F32, 197632, 212224)
    combT = at("combT", [128, TOWN], BF16, 205824, 212224)
    r_cb = at("r_cb", [128, NE], BF16, 209920, 212224)
    xo2 = at("xo2", [128, 2, 512], F32, 109568, 212224)
    ost = at("ost", [128, 2, 512], F32, 113664, 212224)

    def sem(name):
        return Sem(es.enter_context(nc.semaphore(name)))

    PSB = [es.enter_context(nc.psum_tensor(f"ps{i}", [128, 512], F32)) for i in range(8)]
    BANKT = [None] * 8

    qC = sem("qC")
    qG = sem("qG")
    qKV = sem("qKV")
    qM = sem("qM")
    qQt = sem("qQt")
    qXA = sem("qXA")
    qKt = [sem("qKt0"), sem("qKt1")]
    qKS = [sem("qKS0"), sem("qKS1")]
    qQS = [sem("qQS0"), sem("qQS1")]
    qWs = [sem("qWs0"), sem("qWs1")]
    qD = [sem("qD0"), sem("qD1")]
    qXC = [sem("qXC0"), sem("qXC1")]
    qR = [sem("qR0"), sem("qR1")]
    qO = [sem("qO0"), sem("qO1")]
    qSp = sem("qSp")
    qDBG = sem("qDBG")
    sT = sem("sT")
    sV = sem("sV")
    sS = sem("sS")
    mG = sem("mG")
    mY = sem("mY")
    bT = sem("bT")
    bV = sem("bV")
    bS = sem("bS")
    MIL = {"sync": mY, "gpsimd": mG, "tensor": bT, "vector": bV, "scalar": bS}

    DATA = {"tensor": sT, "vector": sV, "scalar": sS}

    def barrier():
        vals = {e: MIL[e].inc() for e in ENGS}
        for e in ENGS:
            others = [(MIL[o], vals[o]) for o in ENGS if o != e]
            selfw = (DATA[e], DATA[e].val) if e in DATA else None

            def _(eng, own=MIL[e], others=others, selfw=selfw):
                if selfw is not None:
                    _w(eng, [selfw])
                eng.nop().then_inc(own.h, 1)
                _w(eng, others)
            P.em(e, _)

    def _(sy):
        for dst, srcap in [(c_ones, ones1[:, :]), (c_onesb, ones1b[:, :]),
                           (c_onec, onec1[:, :]), (c_onecb, onec1b[:, :]),
                           (c_id, identf[:, :]), (c_idb, identb[:, :]),
                           (c_oneh, onehotI[:, :]),
                           (c_onesr, onesr[:, :]), (c_e0, e0m[:, :]),
                           (c_bq, bq2[:, :]), (c_bk, bk2[:, :]),
                           (c_bo, bo2[:, :]), (bv_row, bv[None, :])]:
            sy.dma_start(out=dst[:], in_=srcap).then_inc(qC.h, 16)
    P.em("sync", _)
    qC.inc(13 * 16)
    consts_ld = qC.t()

    def _(g):
        g.dma_start(out=c_gateT[:], in_=gateT.rearrange("(c p) e -> p c e", p=128)).then_inc(qG.h, 16)
    P.em("gpsimd", _)
    qG.inc(16)
    gate_ld = qG.t()

    def _(v, w=[consts_ld]):
        v.memset(c_negC[:], -float(EXP_SHIFT))
        v.memset(bv_rowb[:], 0.0)
        v.memset(l_rb[:, :, :], 0.0)
        _w(v, w)
        v.tensor_scalar_mul(c_bq[:], c_bq[:], 0.125).then_inc(sV.h, 1)
    P.em("vector", _)
    bq_ok = (sV, sV.inc())

    def _(s, w=[consts_ld, bq_ok]):
        _w(s, w)
        s.activation(out=bv_rowb[0:1, :], in_=bv_row[:], func=AF.Identity).then_inc(sS.h, 1)
    P.em("scalar", _)
    bvb_cast = (sS, sS.inc())

    def _(t, w=[bvb_cast, bq_ok]):
        _w(t, w)
        t.matmul(PSB[6][:], c_e0[:, :], bv_rowb[:, :], start=True, stop=True).then_inc(sT.h, 1)
    P.em("tensor", _)
    bvb_mm = (sT, sT.inc())

    def _(s, w=[bvb_mm]):
        _w(s, w)
        s.copy(out=c_bvb[:], in_=PSB[6][:]).then_inc(sS.h, 1)
    P.em("scalar", _)
    bvb_ok = (sS, sS.inc())
    BANKT[6] = bvb_ok

    def emit_ln_stats(src3, ready, bc_dst, banks, src_is_bf16=False,
                      extra_sq_ready=None):
        b0, b1, b2, b3 = banks
        sbase = sS.val

        def _(s, src3=src3, w=[ready, extra_sq_ready], tb=sT.val):
            _w(s, w)
            for hc in range(HC):
                if hc >= 1:
                    s.wait_ge(sT.h, max(tb + hc - 1, 0))
                s.activation(out=l_sq[:, hc % 2, :], in_=src3[:, hc, :], func=AF.Square)\
                    .then_inc(sS.h, 1)
        P.em("scalar", _)
        sS.inc(HC)

        if not src_is_bf16:
            vbase = sV.val

            def _(v, src3=src3, w=[ready], tb=sT.val):
                _w(v, w)
                for hc in range(HC):
                    if hc >= 1:
                        v.wait_ge(sT.h, max(tb + hc - 1, 0))
                    v.tensor_copy(l_xb[:, hc % 2, :], src3[:, hc, :]).then_inc(sV.h, 1)
            P.em("vector", _)
            sV.inc(HC)

        def _(t, src3=src3, w=[ready, BANKT[b0], BANKT[b1]], sbase=sbase,
              vbase=sV.val - HC, src_is_bf16=src_is_bf16):
            _w(t, w)
            for hc in range(HC):
                if src_is_bf16:
                    t.matmul(PSB[b0][0:1, :], c_onecb[:], src3[:, hc, :],
                             start=(hc == 0), stop=(hc == HC - 1))
                else:
                    t.wait_ge(sV.h, vbase + hc + 1)
                    t.matmul(PSB[b0][0:1, :], c_onecb[:], l_xb[:, hc % 2, :],
                             start=(hc == 0), stop=(hc == HC - 1))
                t.wait_ge(sS.h, sbase + hc + 1)
                t.matmul(PSB[b1][0:1, :], c_onecb[:], l_sq[:, hc % 2, :],
                         start=(hc == 0), stop=(hc == HC - 1)).then_inc(sT.h, 1)
        P.em("tensor", _)
        sT.inc(HC)
        stats_mm = sT.t()

        def _(v, w=[stats_mm], base=sV.val):
            _w(v, w)
            k = base
            for fn in [
                lambda: v.tensor_scalar_mul(l_stat[:, 0, :], PSB[b0][0:1, :], 1.0 / D),
                lambda: v.tensor_scalar_mul(l_stat[:, 1, :], PSB[b1][0:1, :], 1.0 / D),
                lambda: v.tensor_mul(l_stat[:, 2, :], l_stat[:, 0, :], l_stat[:, 0, :]),
                lambda: v.tensor_sub(l_stat[:, 1, :], l_stat[:, 1, :], l_stat[:, 2, :]),
                lambda: v.tensor_scalar_add(l_stat[:, 1, :], l_stat[:, 1, :], LN_EPS),
            ]:
                fn().then_inc(sV.h, 1)
                k += 1
                v.wait_ge(sV.h, k)
        P.em("vector", _)
        sV.inc(5)
        var_ok = sV.t()
        BANKT[b0] = var_ok
        BANKT[b1] = var_ok

        def _(s, w=[var_ok]):
            _w(s, w)
            s.activation(out=l_stat[:, 2, :], in_=l_stat[:, 1, :], func=AF.Sqrt)\
                .then_inc(sS.h, 1)
        P.em("scalar", _)
        sqrt_ok = (sS, sS.inc())

        def _(v, w=[sqrt_ok], base=sV.val):
            _w(v, w)
            v.reciprocal(l_stat[:, 2, :], l_stat[:, 2, :]).then_inc(sV.h, 1)
            v.wait_ge(sV.h, base + 1)
            v.tensor_mul(l_stat[:, 3, :], l_stat[:, 0, :], l_stat[:, 2, :])\
                .then_inc(sV.h, 1)
        P.em("vector", _)
        rmul_ok = (sV, sV.inc(2))
        rmul_ok = sV.t()

        def _(s, w=[rmul_ok]):
            _w(s, w)
            s.activation(out=l_rb[0:1, 0, :], in_=l_stat[:, 2, :], func=AF.Identity)
            s.activation(out=l_rb[0:1, 1, :], in_=l_stat[:, 3, :], func=AF.Identity)\
                .then_inc(sS.h, 1)
        P.em("scalar", _)
        rstd_ok = (sS, sS.inc())

        def _(t, w=[rstd_ok, BANKT[b2], BANKT[b3]]):
            _w(t, w)
            t.matmul(PSB[b2][:], c_e0[:, :], l_rb[:, 0, :], start=True, stop=True)
            t.matmul(PSB[b3][:], c_e0[:, :], l_rb[:, 1, :], start=True, stop=True)\
                .then_inc(sT.h, 1)
        P.em("tensor", _)
        bc_mm = (sT, sT.inc())

        def _(s, bc_dst=bc_dst, w=[bc_mm]):
            _w(s, w)
            s.copy(out=bc_dst[:, 0, :], in_=PSB[b2][:])
            s.copy(out=bc_dst[:, 1, :], in_=PSB[b3][:]).then_inc(sS.h, 1)
        P.em("scalar", _)
        bc_ok = (sS, sS.inc())
        BANKT[b2] = bc_ok
        BANKT[b3] = bc_ok
        return bc_ok

    run_A = upto not in ("P0",)
    run_A2 = upto not in ("P0", "A", "ALN", "AK", "AK1", "AK2", "AKE", "AKM", "AVM", "AVE", "AV")
    run_K = upto not in ("ALN", "AV")
    run_V = upto not in ("ALN", "AK", "AK1", "AK2", "AKE", "AKM")
    run_KSWP = upto not in ("AK1", "AKE", "AKM")
    run_KROPE = upto not in ("AK1", "AK2", "AKE", "AKM")
    def _(g):
        for part in range(4):
            g.dma_start(out=wk_sb[:, 4 * part:4 * part + 4, :],
                        in_=wkT.rearrange("(c p) n -> p c n", p=128)[:, 4 * part:4 * part + 4, :]).then_inc(qKV.h, 16)
            g.dma_start(out=wv_sb[:, 4 * part:4 * part + 4, :],
                        in_=wvT.rearrange("(c p) n -> p c n", p=128)[:, 4 * part:4 * part + 4, :]).then_inc(qKV.h, 16)
    P.em("gpsimd", _)
    qKV.inc(8 * 16)
    wkv_ld = qKV.t()

    def _(v, w=[bvb_ok]):
        _w(v, w)
        v.memset(v_sb[:, :, :], 1.0).then_inc(sV.h, 1)
    P.em("vector", _)
    vones_ok = (sV, sV.inc())

    chunk_free = None
    A_RANGE = range(4) if run_A else range(0)
    prevA = {"kevac": [None, None], "kswpdma": None, "kt1": None, "krope": [None, None],
             "vevac": [None, None], "ropeld": [None, None]}
    for tc in A_RANGE:
        def _(sy, tc=tc, w=[chunk_free]):
            _w(sy, w)
            for part in range(4):
                sy.dma_start(
                    out=xf[:, 4 * part:4 * part + 4, :],
                    in_=xT.rearrange("(c p) t -> p c t", p=128)
                        [:, 4 * part:4 * part + 4, 512 * tc:512 * tc + 512],
                ).then_inc(qXA.h, 16)
        P.em("sync", _)
        qXA.inc(64)
        x_ld = qXA.t()

        def _(sy, tc=tc, w=[prevA["ropeld"][tc % 2]]):
            _w(sy, w)
            sy.dma_start(out=ckc[:, tc % 2, :], in_=cosk[:, 512 * tc:512 * tc + 512]).then_inc(qKt[tc % 2].h, 16)
            sy.dma_start(out=skc[:, tc % 2, :], in_=sink[:, 512 * tc:512 * tc + 512]).then_inc(qKt[tc % 2].h, 16)
        P.em("sync", _)
        qKt[tc % 2].inc(32)
        ktab_ld = qKt[tc % 2].t()

        bc_ok = emit_ln_stats(xf, x_ld, l_bc, (0, 1, 2, 3))

        def _(v, w=[bc_ok]):
            _w(v, w)
            for hc in range(HC):
                v.tensor_mul(l_nrm[:], xf[:, hc, :], l_bc[:, 0, :])
                op = v.tensor_sub(lf[:, hc, :], l_nrm[:], l_bc[:, 1, :])
            op.then_inc(sV.h, 1)
        P.em("vector", _)
        ln_ok = (sV, sV.inc())

        for dc in (range(4) if run_K else range(0)):
            bk_ = 4

            def _(t, dc=dc, bk_=bk_, w=[ln_ok, wkv_ld, BANKT[bk_]]):
                _w(t, w)
                for hc in range(HC):
                    mm = t.matmul(PSB[bk_][:], wk_sb[:, hc, 128 * dc:128 * dc + 128],
                                  lf[:, hc, :], start=(hc == 0), stop=(hc == HC - 1))
                mm.then_inc(sT.h, 1)
            P.em("tensor", _)
            kmm = (sT, sT.inc())
            if upto == "AKM":
                prevA["kevac"][dc % 2] = kmm
                BANKT[bk_] = kmm
                prevA["krope"][dc % 2] = kmm
                prevA["kt1"] = kmm
                krope_ok = kmm
                continue

            def _(s, dc=dc, bk_=bk_,
                  w=[kmm, prevA["kswpdma"], prevA["kt1"]]):
                _w(s, w)
                if upto == "AKE":
                    s.copy(out=kraw[:, dc % 2, :], in_=PSB[bk_][:])
                    s.nop().then_inc(sS.h, 1)
                else:
                    s.activation(out=kraw[:, dc % 2, :], in_=PSB[bk_][:], func=AF.Identity,
                                 bias=c_bk[:, dc:dc + 1], scale=1.0).then_inc(sS.h, 1)
            P.em("scalar", _)
            kevac = (sS, sS.inc())
            prevA["kevac"][dc % 2] = kevac
            BANKT[bk_] = kevac

            if run_KSWP:
                def _(sy, dc=dc, w=[kevac, prevA["krope"][dc % 2]]):
                    _w(sy, w)
                    p = dc % 2
                    sy.dma_start(out=kswp[0:32, p, :], in_=kraw[32:64, p, :]).then_inc(qKS[p].h, 16)
                    sy.dma_start(out=kswp[32:64, p, :], in_=kraw[0:32, p, :]).then_inc(qKS[p].h, 16)
                    sy.dma_start(out=kswp[64:96, p, :], in_=kraw[96:128, p, :]).then_inc(qKS[p].h, 16)
                    sy.dma_start(out=kswp[96:128, p, :], in_=kraw[64:96, p, :]).then_inc(qKS[p].h, 16)
                P.em("sync", _)
                qKS[dc % 2].inc(64)
                swp_ok = qKS[dc % 2].t()
                prevA["kswpdma"] = swp_ok
            else:
                swp_ok = None

            if run_KROPE:
                def _(v, tc=tc, dc=dc, w=[kevac, ktab_ld], w2=swp_ok):
                    p = dc % 2
                    _w(v, w)
                    v.tensor_mul(kt1[:], kraw[:, p, :], ckc[:, tc % 2, :]).then_inc(sV.h, 1)
                    _w(v, [w2])
                    v.tensor_mul(kt2[:], kswp[:, p, :], skc[:, tc % 2, :])
                    v.tensor_add(k_sb[:, dc, 512 * tc:512 * tc + 512], kt1[:], kt2[:])\
                        .then_inc(sV.h, 1)
                P.em("vector", _)
                sV.inc()
                prevA["kt1"] = sV.t()
                krope_ok = (sV, sV.inc())
                prevA["krope"][dc % 2] = krope_ok
            else:
                def _(v, tc=tc, dc=dc, w=[kevac] + ([swp_ok] if swp_ok else [])):
                    p = dc % 2
                    _w(v, w)
                    v.tensor_copy(k_sb[:, dc, 512 * tc:512 * tc + 512], kraw[:, p, :])
                    v.nop().then_inc(sV.h, 1).then_inc(sV.h, 0) if False else None
                    v.nop().then_inc(sV.h, 1)
                P.em("vector", _)
                krope_ok = (sV, sV.inc())
                prevA["kt1"] = krope_ok
                prevA["krope"][dc % 2] = krope_ok
        if run_K:
            prevA["ropeld"][tc % 2] = krope_ok

        for ts in (range(4) if run_V else range(0)):
            bk_ = 5

            def _(t, ts=ts, bk_=bk_, w=[ln_ok, wkv_ld, BANKT[bk_]]):
                _w(t, w)
                for hc in range(HC):
                    mm = t.matmul(PSB[bk_][:], lf[:, hc, 128 * ts:128 * ts + 128],
                                  wv_sb[:, hc, :], start=(hc == 0), stop=(hc == HC - 1))
                mm.then_inc(sT.h, 1)
            P.em("tensor", _)
            vmm = (sT, sT.inc())

            if upto == "AVM":
                def _(v, w=[vmm]):
                    _w(v, w)
                    v.nop().then_inc(sV.h, 1)
                P.em("vector", _)
            elif upto == "AVE":
                def _(v, tc=tc, ts=ts, bk_=bk_, w=[vmm, vones_ok]):
                    _w(v, w)
                    for h8 in range(8):
                        v.tensor_add(v_sb[:, 4 * tc + ts, 65 * h8:65 * h8 + 64],
                                     PSB[bk_][:, 64 * h8:64 * h8 + 64],
                                     c_bvb[:, 64 * h8:64 * h8 + 64])
                    v.nop().then_inc(sV.h, 1)
                P.em("vector", _)
            else:
                def _(v, tc=tc, ts=ts, bk_=bk_, w=[vmm, vones_ok]):
                    _w(v, w)
                    v.tensor_add(
                        v_sb[:, 4 * tc + ts, :].rearrange("p (h d) -> p h d", d=65)[:, :, 0:64],
                        PSB[bk_][:].rearrange("p (h d) -> p h d", d=64),
                        c_bvb[:].rearrange("p (h d) -> p h d", d=64),
                    ).then_inc(sV.h, 1)
                P.em("vector", _)
            vevac = (sV, sV.inc())
            BANKT[bk_] = vevac

        chunk_free = sT.t()

    def _(sy):
        sy.dma_start(out=cq_sb[:], in_=cosq[:, :]).then_inc(qQt.h, 16)
        sy.dma_start(out=sq_sb[:], in_=sinq[:, :]).then_inc(qQt.h, 16)
    P.em("sync", _)
    qQt.inc(32)
    qtab_ld = qQt.t()

    qmm_hist = []
    nq = 0
    TH_RANGE = range(2) if run_A2 else range(0)
    prevQ = {"evac": [None, None], "swpdma": None, "qt1": None, "qrope": [None, None]}
    last_chunk_users = chunk_free
    for th in TH_RANGE:
        def _(sy, th=th, w=[last_chunk_users]):
            _w(sy, w)
            for part in range(4):
                sy.dma_start(
                    out=xo[:, 4 * part:4 * part + 4, :],
                    in_=xTo.rearrange("(c p) t -> p c t", p=128)
                        [:, 4 * part:4 * part + 4, 512 * th:512 * th + 512],
                ).then_inc(qXA.h, 16)
        P.em("sync", _)
        qXA.inc(64)
        xo_ld = qXA.t()

        bc_ok = emit_ln_stats(xo, xo_ld, l_bc, (0, 1, 2, 3))

        def _(v, w=[bc_ok]):
            _w(v, w)
            for hc in range(HC):
                v.tensor_mul(l_nrm[:], xo[:, hc, :], l_bc[:, 0, :])
                op = v.tensor_sub(ln1o[:, hc, :], l_nrm[:], l_bc[:, 1, :])
            op.then_inc(sV.h, 1)
        P.em("vector", _)
        lno_ok = (sV, sV.inc())

        for jc in range(HC):
            buf = nq % 2
            bk_ = 4 + nq % 2

            def _(g, th=th, jc=jc, buf=buf, nq=nq,
                  wfree=(qmm_hist[nq - 2] if nq >= 2 else None)):
                if wfree is not None:
                    _w(g, [wfree])
                g.dma_start(
                    out=wpool[:, buf, 0:HC, :],
                    in_=wqT.rearrange("(c p) j -> p c j", p=128)[:, :, 128 * jc:128 * jc + 128],
                ).then_inc(qWs[nq % 2].h, 16)
            P.em("gpsimd", _)
            qWs[nq % 2].inc(16)
            wq_ld = qWs[nq % 2].t()

            def _(t, th=th, jc=jc, buf=buf, bk_=bk_,
                  w=[wq_ld, lno_ok, BANKT[bk_], bq_ok]):
                _w(t, w)
                for hc in range(HC):
                    mm = t.matmul(PSB[bk_][:], wpool[:, buf, hc, :], ln1o[:, hc, :],
                                  start=(hc == 0), stop=(hc == HC - 1))
                mm.then_inc(sT.h, 1)
            P.em("tensor", _)
            qmm = (sT, sT.inc())
            qmm_hist.append(qmm)

            def _(s, jc=jc, bk_=bk_, p=nq % 2,
                  w=[qmm, prevQ["swpdma"], prevQ["qt1"]]):
                _w(s, w)
                s.activation(out=qraw[:, p, :], in_=PSB[bk_][:], func=AF.Identity,
                             bias=c_bq[:, jc:jc + 1], scale=0.125).then_inc(sS.h, 1)
            P.em("scalar", _)
            qevac = (sS, sS.inc())
            BANKT[bk_] = qevac

            def _(sy, p=nq % 2, w=[qevac, prevQ["qrope"][nq % 2]]):
                _w(sy, w)
                sy.dma_start(out=qswp[0:32, p, :], in_=qraw[32:64, p, :]).then_inc(qQS[p].h, 16)
                sy.dma_start(out=qswp[32:64, p, :], in_=qraw[0:32, p, :]).then_inc(qQS[p].h, 16)
                sy.dma_start(out=qswp[64:96, p, :], in_=qraw[96:128, p, :]).then_inc(qQS[p].h, 16)
                sy.dma_start(out=qswp[96:128, p, :], in_=qraw[64:96, p, :]).then_inc(qQS[p].h, 16)
            P.em("sync", _)
            qQS[nq % 2].inc(64)
            swp_ok = qQS[nq % 2].t()
            prevQ["swpdma"] = swp_ok

            def _(v, th=th, jc=jc, p=nq % 2, w=[qevac, qtab_ld], w2=swp_ok):
                _w(v, w)
                v.tensor_mul(qt1[:], qraw[:, p, :], cq_sb[:, 512 * th:512 * th + 512])\
                    .then_inc(sV.h, 1)
                _w(v, [w2])
                v.tensor_mul(qt2[:], qswp[:, p, :], sq_sb[:, 512 * th:512 * th + 512])
                v.tensor_add(q_sb[:, jc, 512 * th:512 * th + 512], qt1[:], qt2[:])\
                    .then_inc(sV.h, 1)
            P.em("vector", _)
            sV.inc()
            prevQ["qt1"] = sV.t()
            qrope_ok = (sV, sV.inc())
            prevQ["qrope"][nq % 2] = qrope_ok
            nq += 1
        last_chunk_users = sT.t()

    barrier()

    if upto in ("B", "C", "E", "DR", "DR1", "CW"):
        def _(sy):
            for g4 in range(4):
                sy.dma_start(out=m_sb[:, 6 * g4:6 * g4 + 6, :],
                             in_=masksI.rearrange("s p t -> p s t")[:, 6 * g4:6 * g4 + 6, :]).then_inc(qM.h, 16)
        P.em("sync", _)
        qM.inc(64)
        mask_ld = qM.t()

        def _(v):
            v.memset(rrowb[:, :, :], 0.0).then_inc(sV.h, 1)
        P.em("vector", _)
        sV.inc()

        pt_owner = [None] * 8
        rrow_guard = [None, None]
        ptc = 0
        for blk in range(2):
            nslot = BLK_NSLOT[blk]
            for cp in range(16):
                hA, hB = EVENKV[cp], ODDKV[cp]
                khA, khB = hA // 4, hB // 4
                pts = {}
                msk_ok = {}

                def emit_s_exp(si, hs, kh):
                    nonlocal ptc
                    bk_ = hs * 2 + si % 2
                    base = (kh % 2) * 64

                    def _(t, blk=blk, cp=cp, si=si, kh=kh, base=base, bk_=bk_,
                          w=[BANKT[bk_], mask_ld]):
                        _w(t, w)
                        t.matmul(PSB[bk_][:],
                                 k_sb[base:base + 64, kh // 2, 128 * si:128 * si + 128],
                                 q_sb[base:base + 64, cp, 512 * blk:512 * blk + 512],
                                 start=True, stop=True).then_inc(sT.h, 1)
                    P.em("tensor", _)
                    smm = (sT, sT.inc())

                    slot = ptc % 8
                    ptc += 1

                    def _(s, bk_=bk_, slot=slot, w=[smm, pt_owner[slot]]):
                        _w(s, w)
                        s.activation(out=pt[:, slot, :], in_=PSB[bk_][:], func=AF.Exp,
                                     bias=c_negC[:], scale=1.0).then_inc(sS.h, 1)
                    P.em("scalar", _)
                    e_ok = (sS, sS.inc())
                    BANKT[bk_] = e_ok

                    mi = _mask_idx(blk, si)
                    if mi is not None:
                        def _(v, slot=slot, mi=mi, w=[e_ok]):
                            _w(v, w)
                            v.tensor_mul(pt[:, slot, :], pt[:, slot, :],
                                         m_sb[:, mi, :]).then_inc(sV.h, 1)
                        P.em("vector", _)
                        msk_ok[(si, hs)] = (sV, sV.inc())
                    else:
                        msk_ok[(si, hs)] = e_ok
                    pts[(si, hs)] = slot

                def emit_av(sj, hs, kh):
                    slot = pts[(sj, hs)]
                    obk = 4 + hs

                    def _(t, sj=sj, kh=kh, slot=slot, obk=obk, nslot=nslot,
                          w=[msk_ok[(sj, hs)]] + ([BANKT[obk]] if sj == 0 else [])):
                        _w(t, w)
                        t.matmul(PSB[obk][0:65, :], v_sb[:, sj, 65 * kh:65 * kh + 65],
                                 pt[:, slot, :], start=(sj == 0), stop=(sj == nslot - 1))\
                            .then_inc(sT.h, 1)
                    P.em("tensor", _)
                    av = (sT, sT.inc())
                    pt_owner[slot] = av
                    return av

                av_last = {}
                for si in range(nslot):
                    emit_s_exp(si, 0, khA)
                    emit_s_exp(si, 1, khB)
                    if si >= 1:
                        av_last[0] = emit_av(si - 1, 0, khA)
                        av_last[1] = emit_av(si - 1, 1, khB)
                av_last[0] = emit_av(nslot - 1, 0, khA)
                av_last[1] = emit_av(nslot - 1, 1, khB)

                for hs, kh in ((0, khA), (1, khB)):
                    obk = 4 + hs

                    def _(v, hs=hs, obk=obk, w=[av_last[hs], rrow_guard[hs]],
                          base=sV.val):
                        _w(v, w)
                        v.reciprocal(rrow[:, hs, :], PSB[obk][64:65, :]).then_inc(sV.h, 1)
                        v.wait_ge(sV.h, base + 1)
                        v.tensor_copy(rrowb[0:1, hs, :], rrow[:, hs, :]).then_inc(sV.h, 1)
                    P.em("vector", _)
                    sV.inc(2)
                    r_ok = sV.t()

                    def _(t, hs=hs, w=[r_ok, BANKT[6]]):
                        _w(t, w)
                        t.matmul(PSB[6][64 * hs:64 * hs + 64, :], c_e0[:, 0:64],
                                 rrowb[:, hs, :], start=True, stop=True).then_inc(sT.h, 1)
                    P.em("tensor", _)
                    rb_mm = (sT, sT.inc())
                    rrow_guard[hs] = rb_mm

                    def _(s, hs=hs, w=[rb_mm, BANKT[4 + hs]]):
                        _w(s, w)
                        s.copy(out=rb[:, hs, :], in_=PSB[6][64 * hs:64 * hs + 64, :])\
                            .then_inc(sS.h, 1)
                    P.em("scalar", _)
                    rb_ok = (sS, sS.inc())
                    BANKT[6] = rb_ok

                    def _(v, blk=blk, cp=cp, hs=hs, obk=obk, w=[rb_ok]):
                        _w(v, w)
                        v.tensor_mul(o_sb[64 * hs:64 * hs + 64, cp, 512 * blk:512 * blk + 512],
                                     PSB[obk][0:64, :], rb[:, hs, :]).then_inc(sV.h, 1)
                    P.em("vector", _)
                    BANKT[obk] = (sV, sV.inc())

        barrier()

    if upto in ("C", "E", "DR", "DR1", "CW"):
        womm_hist = []
        for jc in range(HC):
            buf = jc % 2

            def _(g, jc=jc, buf=buf,
                  wfree=(womm_hist[jc - 2] if jc >= 2 else None)):
                if wfree is not None:
                    _w(g, [wfree])
                g.dma_start(
                    out=wpool[:, buf, 0:HC, :],
                    in_=woT.rearrange("(c p) j -> p c j", p=128)[:, :, 128 * jc:128 * jc + 128],
                ).then_inc(qWs[jc % 2].h, 16)
            P.em("gpsimd", _)
            qWs[jc % 2].inc(16)
            wo_ld = qWs[jc % 2].t()

            for th2 in range(2):
                bk_ = 2 * buf + th2

                def _(t, jc=jc, buf=buf, th2=th2, bk_=bk_, w=[wo_ld, BANKT[bk_]]):
                    _w(t, w)
                    for hc in range(HC):
                        mm = t.matmul(PSB[bk_][:], wpool[:, buf, hc, :],
                                      o_sb[:, hc, 512 * th2:512 * th2 + 512],
                                      start=(hc == 0), stop=(hc == HC - 1))
                    mm.then_inc(sT.h, 1)
                P.em("tensor", _)
                ymm = (sT, sT.inc())
                if th2 == 1:
                    womm_hist.append(ymm)

                def _(s, jc=jc, th2=th2, bk_=bk_, w=[ymm]):
                    _w(s, w)
                    s.activation(out=yatt[:, jc, 512 * th2:512 * th2 + 512],
                                 in_=PSB[bk_][:], func=AF.Identity,
                                 bias=c_bo[:, jc:jc + 1], scale=1.0).then_inc(sS.h, 1)
                P.em("scalar", _)
                BANKT[bk_] = (sS, sS.inc())
        yatt_ok = sS.t()

        if debug and upto in ("DR1", "CW"):
            yatt_cap = at("yatt_cap", [128, 16, 512], F32, 142336, 212224)
            dbg["yatt_cap"] = yatt_cap

            def _(v, w=[yatt_ok]):
                _w(v, w)
                for hc in range(HC):
                    v.tensor_copy(yatt_cap[:, hc, :], yatt[:, hc, 0:512])
                v.nop().then_inc(sV.h, 1)
            P.em("vector", _)
            sV.inc()

        x2_spill = []
        for th in (range(2) if upto != "CW" else range(0)):
            bcA = emit_ln_stats(yatt[:, :, 512 * th:512 * th + 512], yatt_ok,
                                l_bc, (0, 1, 2, 3))
            if debug and upto == "DR1" and th == 0:
                lbc_cap = at("lbc_cap", [128, 2, 512], F32, 109568, 212224)
                lrb_cap = at("lrb_cap", [1, 2, 512], F32, 126016, 212224)

                lst_cap = at("lst_cap", [1, 4, 512], F32, 130112, 212224)

                def _(v, w=[bcA]):
                    _w(v, w)
                    v.tensor_copy(lbc_cap[:, 0, :], l_bc[:, 0, :])
                    v.tensor_copy(lbc_cap[:, 1, :], l_bc[:, 1, :])
                    v.tensor_copy(lrb_cap[:, 0, :], l_rb[0:1, 0, :])
                    v.tensor_copy(lrb_cap[:, 1, :], l_rb[0:1, 1, :])
                    for rr_ in range(4):
                        v.tensor_copy(lst_cap[:, rr_, :], l_stat[:, rr_, :])
                    v.nop().then_inc(sV.h, 1)
                P.em("vector", _)
                lbc_ok = (sV, sV.inc())
                dbg["lbc_ok"] = lbc_ok
                dbg["lbc_cap"] = lbc_cap
                dbg["lrb_cap"] = lrb_cap
                dbg["lst_cap"] = lst_cap
            xoc_tk = [None, None]
            sq2base = sS.val
            x2_ok = []
            for hc in range(HC):
                def _(sy, th=th, hc=hc, w=[yatt_ok, xoc_tk[hc % 2]]):
                    _w(sy, w)
                    sy.dma_start(
                        out=xoc[:, hc % 2, :],
                        in_=xTo.rearrange("(c p) t -> p c t", p=128)
                            [:, hc, 512 * th:512 * th + 512],
                    ).then_inc(qXC[hc % 2].h, 16)
                P.em("sync", _)
                qXC[hc % 2].inc(16)
                xoc_ld = qXC[hc % 2].t()

                def _(v, th=th, hc=hc, w=[bcA, xoc_ld]):
                    _w(v, w)
                    sl = yatt[:, hc, 512 * th:512 * th + 512]
                    v.tensor_mul(l_nrm[:], sl, l_bc[:, 0, :])
                    v.tensor_sub(l_nrm[:], l_nrm[:], l_bc[:, 1, :])
                    v.tensor_add(sl, l_nrm[:], xoc[:, hc % 2, :]).then_inc(sV.h, 1)
                P.em("vector", _)
                x2c_ok = (sV, sV.inc())
                xoc_tk[hc % 2] = x2c_ok
                x2_ok.append(x2c_ok)

                def _(sy, th=th, hc=hc, w=[x2c_ok]):
                    _w(sy, w)
                    sy.dma_start(
                        out=x2s.rearrange("(c p) t -> p c t", p=128)[:, hc, 512 * th:512 * th + 512],
                        in_=yatt[:, hc, 512 * th:512 * th + 512]).then_inc(qSp.h, 16)
                P.em("sync", _)
                qSp.inc(16)
                x2_spill.append(qSp.t())

                def _(s, th=th, hc=hc, w=[x2c_ok], tb=sT.val):
                    _w(s, w)
                    if hc >= 1:
                        s.wait_ge(sT.h, tb)
                    s.activation(out=l_sq[:, hc % 2, :],
                                 in_=yatt[:, hc, 512 * th:512 * th + 512],
                                 func=AF.Square).then_inc(sS.h, 1)
                P.em("scalar", _)
                sq_ok = (sS, sS.inc())

                def _(v, th=th, hc=hc, w=[x2c_ok], tb=sT.val):
                    _w(v, w)
                    if hc >= 1:
                        v.wait_ge(sT.h, tb)
                    v.tensor_copy(l_xb[:, hc % 2, :],
                                  yatt[:, hc, 512 * th:512 * th + 512]).then_inc(sV.h, 1)
                P.em("vector", _)
                xb_ok = (sV, sV.inc())

                def _(t, th=th, hc=hc, w=[xb_ok, sq_ok] + ([BANKT[4], BANKT[5]] if hc == 0 else [])):
                    _w(t, w)
                    t.matmul(PSB[4][0:1, :], c_onecb[:], l_xb[:, hc % 2, :],
                             start=(hc == 0), stop=(hc == HC - 1))
                    t.matmul(PSB[5][0:1, :], c_onecb[:], l_sq[:, hc % 2, :],
                             start=(hc == 0), stop=(hc == HC - 1)).then_inc(sT.h, 1)
                P.em("tensor", _)
                sT.inc()
            ln2stats_mm = sT.t()

            def _(v, w=[ln2stats_mm], base=sV.val):
                _w(v, w)
                k = base
                for fn in [
                    lambda: v.tensor_scalar_mul(l_stat[:, 0, :], PSB[4][0:1, :], 1.0 / D),
                    lambda: v.tensor_scalar_mul(l_stat[:, 1, :], PSB[5][0:1, :], 1.0 / D),
                    lambda: v.tensor_mul(l_stat[:, 2, :], l_stat[:, 0, :], l_stat[:, 0, :]),
                    lambda: v.tensor_sub(l_stat[:, 1, :], l_stat[:, 1, :], l_stat[:, 2, :]),
                    lambda: v.tensor_scalar_add(l_stat[:, 1, :], l_stat[:, 1, :], LN_EPS),
                ]:
                    fn().then_inc(sV.h, 1)
                    k += 1
                    v.wait_ge(sV.h, k)
            P.em("vector", _)
            sV.inc(5)
            var2_ok = sV.t()
            BANKT[4] = var2_ok
            BANKT[5] = var2_ok

            def _(s, w=[var2_ok]):
                _w(s, w)
                s.activation(out=l_stat[:, 2, :], in_=l_stat[:, 1, :], func=AF.Sqrt)\
                    .then_inc(sS.h, 1)
            P.em("scalar", _)
            sqrt2_ok = (sS, sS.inc())

            def _(v, w=[sqrt2_ok], base=sV.val):
                _w(v, w)
                v.reciprocal(l_stat[:, 2, :], l_stat[:, 2, :]).then_inc(sV.h, 1)
                v.wait_ge(sV.h, base + 1)
                v.tensor_mul(l_stat[:, 3, :], l_stat[:, 0, :], l_stat[:, 2, :])\
                    .then_inc(sV.h, 1)
            P.em("vector", _)
            sV.inc(2)
            rmul2_ok = sV.t()

            def _(s, w=[rmul2_ok]):
                _w(s, w)
                s.activation(out=l_rb[0:1, 0, :], in_=l_stat[:, 2, :], func=AF.Identity)
                s.activation(out=l_rb[0:1, 1, :], in_=l_stat[:, 3, :], func=AF.Identity)\
                    .then_inc(sS.h, 1)
            P.em("scalar", _)
            rstd2_ok = (sS, sS.inc())

            def _(t, w=[rstd2_ok, BANKT[2], BANKT[3]]):
                _w(t, w)
                t.matmul(PSB[2][:], c_e0[:, :], l_rb[:, 0, :], start=True, stop=True)
                t.matmul(PSB[3][:], c_e0[:, :], l_rb[:, 1, :], start=True, stop=True)\
                    .then_inc(sT.h, 1)
            P.em("tensor", _)
            bc2_mm = (sT, sT.inc())

            def _(s, w=[bc2_mm]):
                _w(s, w)
                s.copy(out=l_bc2[:, 0, :], in_=PSB[2][:])
                s.copy(out=l_bc2[:, 1, :], in_=PSB[3][:]).then_inc(sS.h, 1)
            P.em("scalar", _)
            bc2_ok = (sS, sS.inc())
            BANKT[2] = bc2_ok
            BANKT[3] = bc2_ok

            def _(v, th=th, w=[bc2_ok]):
                _w(v, w)
                for hc in range(HC):
                    v.tensor_mul(l_nrm[:], yatt[:, hc, 512 * th:512 * th + 512], l_bc2[:, 0, :])
                    op = v.tensor_sub(ln2_sb[:, hc, 512 * th:512 * th + 512], l_nrm[:], l_bc2[:, 1, :])
                op.then_inc(sV.h, 1)
            P.em("vector", _)
            sV.inc()
        ln2_ok = sV.t()

        barrier()
    if upto == "CW":
        pass

    if upto in ("E", "DR", "DR1"):
        def _(v, w=[ln2_ok]):
            _w(v, w)
            v.memset(combT[:, :], 0.0).then_inc(sV.h, 1)
        P.em("vector", _)
        combT_z = (sV, sV.inc())

        comb_ok = []
        prev_ex = None
        prev_tr = None
        RB = PSB[0] if upto == "DR1" else PSB[6]
        for tsub in range(8):
            def _(t, tsub=tsub, w=[ln2_ok, gate_ld, BANKT[6]]):
                _w(t, w)
                for hc in range(HC):
                    mm = t.matmul(RB[:, 0:NE], ln2_sb[:, hc, 128 * tsub:128 * tsub + 128],
                                  c_gateT[:, hc, :], start=(hc == 0), stop=(hc == HC - 1))
                mm.then_inc(sT.h, 1)
            P.em("tensor", _)
            lg_mm = (sT, sT.inc())

            if upto == "DR1":
                def _(v, w=[lg_mm, prev_ex]):
                    _w(v, w)
                    v.tensor_copy(r_sc[:, 0, :], RB[:, 0:NE]).then_inc(sV.h, 1)
                P.em("vector", _)
                nm_ok = (sV, sV.inc())
            else:
                def _(v, w=[lg_mm, prev_ex]):
                    _w(v, w)
                    v.reduce_max(r_m[:, 0:1], RB[:, 0:NE], axis=AX.X, negate=True)\
                        .then_inc(sV.h, 1)
                P.em("vector", _)
                nm_ok = (sV, sV.inc())

            def _(s, w=[nm_ok], skip=(upto == "DR1")):
                _w(s, w)
                if skip:
                    s.nop().then_inc(sS.h, 1)
                else:
                    s.activation(out=r_sc[:, 0, :], in_=RB[:, 0:NE], func=AF.Exp,
                                 bias=r_m[:, 0:1], scale=1.0).then_inc(sS.h, 1)
            P.em("scalar", _)
            ex_ok = (sS, sS.inc())
            prev_ex = ex_ok
            BANKT[6] = ex_ok

            def _(v, tsub=tsub, w=[ex_ok], skip=(upto == "DR1"), base=sV.val):
                _w(v, w)
                ops = [
                    lambda: v.nop() if skip else v.reduce_sum(r_m[:, 1:2], r_sc[:, 0, :], axis=AX.X),
                    lambda: v.nop() if skip else v.reciprocal(r_m[:, 1:2], r_m[:, 1:2]),
                    lambda: v.nop() if skip else v.tensor_scalar_mul(r_sc[:, 0, :], r_sc[:, 0, :], r_m[:, 1:2]),
                    lambda: v.nop() if skip else v.reduce_max(r_m[:, 2:3], r_sc[:, 0, :], axis=AX.X),
                    lambda: v.nop() if skip else v.tensor_single_scalar(r_sc[:, 1, :], r_sc[:, 0, :], r_m[:, 2:3], op=ALU.is_ge),
                    lambda: v.nop() if skip else v.tensor_scalar_mul(r_sc[:, 1, :], r_sc[:, 1, :], 1.0e30),
                    lambda: v.nop() if skip else v.tensor_sub(r_sc[:, 1, :], r_sc[:, 0, :], r_sc[:, 1, :]),
                    lambda: v.nop() if skip else v.reduce_max(r_m[:, 3:4], r_sc[:, 1, :], axis=AX.X),
                    lambda: v.nop() if skip else v.tensor_single_scalar(r_sc[:, 1, :], r_sc[:, 0, :], r_m[:, 3:4], op=ALU.is_ge),
                    lambda: v.nop() if skip else v.tensor_mul(r_sc[:, 0, :], r_sc[:, 0, :], r_sc[:, 1, :]),
                ]
                k = base
                for fn in ops:
                    fn().then_inc(sV.h, 1)
                    k += 1
                    v.wait_ge(sV.h, k)
            P.em("vector", _)
            sV.inc(10)
            cmb_ok = sV.t()

            def _(v, tsub=tsub, w=[cmb_ok, prev_tr], dodiag=(debug and upto == "DR1")):
                _w(v, w)
                v.tensor_copy(r_cb[:, :], r_sc[:, 0, :]).then_inc(sV.h, 1)
                if dodiag:
                    v.tensor_copy(st1.rearrange("p (a b) -> p a b", b=16)[:, tsub, :],
                                  r_sc[:, 0, :])
                    v.tensor_copy(st2.rearrange("p (a b) -> p a b", b=16)[:, tsub, :],
                                  r_cb[:, :])
            P.em("vector", _)
            cbb_ok = (sV, sV.inc())

            def _(t, tsub=tsub, w=[cbb_ok, BANKT[7]]):
                _w(t, w)
                t.matmul(PSB[7][0:NE, 0:128], r_cb[:, :], c_idb[:],
                         start=True, stop=True).then_inc(sT.h, 1)
            P.em("tensor", _)
            tr_mm = (sT, sT.inc())
            prev_tr = tr_mm

            def _(v, tsub=tsub, w=[tr_mm]):
                _w(v, w)
                v.tensor_copy(combT[0:NE, 128 * tsub:128 * tsub + 128], PSB[7][0:NE, 0:128])\
                    .then_inc(sV.h, 1)
            P.em("vector", _)
            ct_ok = (sV, sV.inc())
            BANKT[7] = ct_ok
            comb_ok.append(ct_ok)

        gumm_hist = []
        dnmm_hist = []
        st1_guard = [None, None]
        ngu = 0
        ndn = 0
        h_ok = {}
        h_users = {}
        cb_ok = {}
        cb_users = {}
        dn_evac = {}

        def emit_gu(u):
            nonlocal ngu
            is_sh = u < 2
            e = u - 2
            hbuf = h_bu[u % 2]
            hck = []
            for jc in range(11):
                bufg = 2 * (ngu % 2)
                bufu = bufg + 1

                def _(g, u=u, e=e, jc=jc, is_sh=is_sh, bufg=bufg, bufu=bufu, ngu=ngu,
                      wfree=(gumm_hist[ngu - 2] if ngu >= 2 else None)):
                    if wfree is not None:
                        _w(g, [wfree])
                    if is_sh:
                        gsrc = sgu.rearrange("(c p) j -> p c j", p=128)
                        g0 = 1408 * u + 128 * jc
                        u0 = SI + 1408 * u + 128 * jc
                    else:
                        gsrc = wgu[e].rearrange("(c p) j -> p c j", p=128)
                        g0 = 128 * jc
                        u0 = I_FF + 128 * jc
                    for part in range(4):
                        g.dma_start(out=wpool[:, bufg, 4 * part:4 * part + 4, :],
                                    in_=gsrc[:, 4 * part:4 * part + 4, g0:g0 + 128]).then_inc(qWs[ngu % 2].h, 16)
                    for part in range(4):
                        g.dma_start(out=wpool[:, bufu, 4 * part:4 * part + 4, :],
                                    in_=gsrc[:, 4 * part:4 * part + 4, u0:u0 + 128]).then_inc(qWs[ngu % 2].h, 16)
                P.em("gpsimd", _)
                qWs[ngu % 2].inc(8 * 16)
                guw_ld = qWs[ngu % 2].t()

                for th2 in range(2):
                    bkg = (2 * jc + th2) % 2
                    bku = 2 + (2 * jc + th2) % 2

                    def _(t, jc=jc, th2=th2, bufg=bufg, bufu=bufu, bkg=bkg, bku=bku,
                          w=[guw_ld, ln2_ok, BANKT[bkg], BANKT[bku]]):
                        _w(t, w)
                        for hc in range(HC):
                            t.matmul(PSB[bkg][:], wpool[:, bufg, hc, :],
                                     ln2_sb[:, hc, 512 * th2:512 * th2 + 512],
                                     start=(hc == 0), stop=(hc == HC - 1))
                        for hc in range(HC):
                            mm = t.matmul(PSB[bku][:], wpool[:, bufu, hc, :],
                                          ln2_sb[:, hc, 512 * th2:512 * th2 + 512],
                                          start=(hc == 0), stop=(hc == HC - 1))
                        mm.then_inc(sT.h, 1)
                    P.em("tensor", _)
                    gu_mm = (sT, sT.inc())
                    if th2 == 1:
                        gumm_hist.append(gu_mm)

                    def _(s, th2=th2, bkg=bkg, w=[gu_mm, st1_guard[th2]]):
                        _w(s, w)
                        s.activation(out=st1[:, 512 * th2:512 * th2 + 512],
                                     in_=PSB[bkg][:], func=AF.Silu).then_inc(sS.h, 1)
                    P.em("scalar", _)
                    silu_ok = (sS, sS.inc())
                    BANKT[bkg] = silu_ok

                    def _(v, u=u, jc=jc, th2=th2, bku=bku, is_sh=is_sh, hbuf=hbuf,
                          w=[silu_ok, h_users.get(u - 2), cb_ok.get(u)]):
                        _w(v, w)
                        sl = slice(512 * th2, 512 * th2 + 512)
                        v.tensor_mul(st2[:, sl], st1[:, sl], PSB[bku][:, :])
                        if is_sh:
                            op = v.tensor_copy(hbuf[:, jc, sl], st2[:, sl])
                        else:
                            op = v.tensor_mul(hbuf[:, jc, sl], st2[:, sl], cb[:, u % 2, sl])
                        op.then_inc(sV.h, 1)
                    P.em("vector", _)
                    hc_ok = (sV, sV.inc())
                    st1_guard[th2] = hc_ok
                    BANKT[bku] = hc_ok
                    if not is_sh:
                        cb_users[u] = hc_ok
                    hck.append(hc_ok)
                ngu += 1
            h_ok[u] = hck[-1]

        def emit_dn(u):
            nonlocal ndn
            is_sh = u < 2
            e = u - 2
            hbuf = h_bu[u % 2]
            last_mm = None
            for hcs in range(8):
                bufd = ndn % 2

                def _(g, u=u, e=e, hcs=hcs, is_sh=is_sh, bufd=bufd, ndn=ndn,
                      wfree=(dnmm_hist[ndn - 2] if ndn >= 2 else None)):
                    if wfree is not None:
                        _w(g, [wfree])
                    if is_sh:
                        dsrc = sdn.rearrange("(c p) h -> p c h", p=128)[:, 11 * u:11 * u + 11, :]
                    else:
                        dsrc = wdn[e].rearrange("(c p) h -> p c h", p=128)
                    for part in range(2):
                        g.dma_start(
                            out=dpool[:, bufd, :, 128 * part:128 * part + 128],
                            in_=dsrc[:, :, 256 * hcs + 128 * part:256 * hcs + 128 * part + 128],
                        ).then_inc(qD[ndn % 2].h, 16)
                P.em("gpsimd", _)
                qD[ndn % 2].inc(32)
                dnw_ld = qD[ndn % 2].t()

                for part in range(2):
                    hc = 2 * hcs + part
                    for th2 in range(2):
                        bkd = 4 + (2 * hc + th2) % 2

                        def _(t, u=u, hc=hc, part=part, th2=th2, bufd=bufd, bkd=bkd,
                              hbuf=hbuf, w=[dnw_ld, h_ok[u], BANKT[bkd]]):
                            _w(t, w)
                            for ic in range(11):
                                mm = t.matmul(PSB[bkd][:], dpool[:, bufd, ic, 128 * part:128 * part + 128],
                                              hbuf[:, ic, 512 * th2:512 * th2 + 512],
                                              start=(ic == 0), stop=(ic == 10))
                            mm.then_inc(sT.h, 1)
                        P.em("tensor", _)
                        dn_mm = (sT, sT.inc())
                        last_mm = dn_mm

                        def _(v, u=u, hc=hc, th2=th2, bkd=bkd,
                              w=[dn_mm] + ([x2_spill[-1]] if u == 0 else []),
                              first=(u == 0)):
                            _w(v, w)
                            sl = slice(512 * th2, 512 * th2 + 512)
                            if first:
                                op = v.tensor_copy(y_sb[:, hc, sl], PSB[bkd][:])
                            else:
                                op = v.tensor_add(y_sb[:, hc, sl], PSB[bkd][:], y_sb[:, hc, sl])
                            op.then_inc(sV.h, 1)
                        P.em("vector", _)
                        BANKT[bkd] = (sV, sV.inc())
                dnmm_hist.append(last_mm)
                ndn += 1
            h_users[u] = last_mm
            dn_evac[u] = sV.t()

        def emit_cb(u):
            e = u - 2
            for th2 in range(2):
                def _(t, e=e, th2=th2, w=[comb_ok[-1], BANKT[6], cb_users.get(u - 2),
                                           combT_z]):
                    _w(t, w)
                    t.matmul(PSB[6][:], c_oneh[:, 128 * e:128 * e + 128],
                             combT[:, 512 * th2:512 * th2 + 512],
                             start=True, stop=True).then_inc(sT.h, 1)
                P.em("tensor", _)
                cbmm = (sT, sT.inc())

                def _(s, u=u, th2=th2, w=[cbmm]):
                    _w(s, w)
                    s.copy(out=cb[:, u % 2, 512 * th2:512 * th2 + 512], in_=PSB[6][:])\
                        .then_inc(sS.h, 1)
                P.em("scalar", _)
                cbc = (sS, sS.inc())
                BANKT[6] = cbc
            cb_ok[u] = cbc

        def emit_gu2(u):
            if u >= 2:
                emit_cb(u)
            emit_gu(u)

        UNITS = range(18) if upto == "E" else range(0)
        for u in UNITS:
            emit_gu2(u)
            if u >= 1:
                emit_dn(u - 1)
        if upto == "E":
            emit_dn(17)
        y_ok = sV.t()

        barrier()

    if upto == "E":
        out_dma = None
        for th in range(2):
            bcF = emit_ln_stats(y_sb[:, :, 512 * th:512 * th + 512], y_ok,
                                l_bc, (0, 1, 2, 3), src_is_bf16=True)
            xo2_tk = [None, None]
            ost_tk = [None, None]
            for hc in range(HC):
                def _(sy, th=th, hc=hc, w=[x2_spill[-1], xo2_tk[hc % 2]]):
                    _w(sy, w)
                    sy.dma_start(
                        out=xo2[:, hc % 2, :],
                        in_=x2s.rearrange("(c p) t -> p c t", p=128)[:, hc, 512 * th:512 * th + 512],
                    ).then_inc(qR[hc % 2].h, 16)
                P.em("sync", _)
                qR[hc % 2].inc(16)
                x2_ld = qR[hc % 2].t()

                def _(v, th=th, hc=hc, w=[bcF, x2_ld, ost_tk[hc % 2]]):
                    _w(v, w)
                    v.tensor_mul(l_nrm[:], y_sb[:, hc, 512 * th:512 * th + 512], l_bc[:, 0, :])
                    v.tensor_sub(l_nrm[:], l_nrm[:], l_bc[:, 1, :])
                    v.tensor_add(ost[:, hc % 2, :], l_nrm[:], xo2[:, hc % 2, :]).then_inc(sV.h, 1)
                P.em("vector", _)
                o_ok = (sV, sV.inc())
                xo2_tk[hc % 2] = o_ok

                def _(sy, th=th, hc=hc, w=[o_ok]):
                    _w(sy, w)
                    sy.dma_start(
                        out=out.rearrange("(c p) t -> p c t", p=128)[:, hc, 512 * th:512 * th + 512],
                        in_=ost[:, hc % 2, :]).then_inc(qO[hc % 2].h, 16)
                P.em("sync", _)
                qO[hc % 2].inc(16)
                out_dma = qO[hc % 2].t()
                ost_tk[hc % 2] = out_dma

    if upto != "E":
        def _(v, w=[sT.t(), sV.t(), sS.t()]):
            _w(v, w)
            v.memset(l_nrm[:], 0.0).then_inc(sV.h, 1)
        P.em("vector", _)
        z_ok = (sV, sV.inc())

        def _(sy, w=[z_ok]):
            _w(sy, w)
            for cc in range(HC):
                for th in range(2):
                    sy.dma_start(
                        out=out.rearrange("(c p) t -> p c t", p=128)[:, cc, 512 * th:512 * th + 512],
                        in_=l_nrm[:]).then_inc(qO[0].h, 16)
        P.em("sync", _)
        qO[0].inc(16 * HC * 2)
        out_dma = qO[0].t()

    if debug:
        have_kv = upto in ("A", "A2", "B")
        have_q = upto in ("A2", "B")
        have_o = upto in ("B", "CW")
        ndbg = (9 if have_kv else 0) + (1 if have_q else 0) + (1 if have_o else 0) \
            + (4 if upto in ("E", "DR", "DR1") else 0) + (6 if upto == "DR1" else 0) \
            + (1 if upto == "CW" else 0)

        def _(g, w=[out_dma, sT.t(), sV.t(), sS.t()]):
            _w(g, w)
            if have_kv:
                g.dma_start(out=dbg["d_k"].rearrange("(c p) t -> p c t", p=128),
                            in_=k_sb[:, :, :]).then_inc(qDBG.h, 16)
                for h8 in range(8):
                    g.dma_start(
                        out=dbg["d_v"].rearrange("(c p) d -> p c d", p=128)[:, :, 64 * h8:64 * h8 + 64],
                        in_=v_sb[:, :, 65 * h8:65 * h8 + 64],
                    ).then_inc(qDBG.h, 16)
            if have_q:
                g.dma_start(out=dbg["d_q"].rearrange("(c p) t -> p c t", p=128),
                            in_=q_sb[:, :, :]).then_inc(qDBG.h, 16)
            if have_o:
                g.dma_start(out=dbg["d_o"].rearrange("(c p) t -> p c t", p=128),
                            in_=o_sb[:, :, :]).then_inc(qDBG.h, 16)
            if upto in ("DR1", "CW"):
                if "yatt_cap" in dbg:
                    g.dma_start(out=dbg["d_yatt"].rearrange("(c p) t -> p c t", p=128),
                                in_=dbg["yatt_cap"][:, :, :]).then_inc(qDBG.h, 16)
                if "lbc_cap" in dbg:
                    g.dma_start(out=dbg["d_lbc"][:, :, :], in_=dbg["lbc_cap"][:, :, :]).then_inc(qDBG.h, 16)
                    g.dma_start(out=dbg["d_lrb"][:, :, :], in_=dbg["lrb_cap"][:, :, :]).then_inc(qDBG.h, 16)
                    g.dma_start(out=dbg["d_lst"][:, :, :], in_=dbg["lst_cap"][:, :, :]).then_inc(qDBG.h, 16)
                g.dma_start(out=dbg["d_rsc"][:, :, :],
                            in_=st1.rearrange("p (a b) -> p a b", b=16)[:, 0:8, :]).then_inc(qDBG.h, 16)
                g.dma_start(out=dbg["d_rcb"][:, :, :],
                            in_=st2.rearrange("p (a b) -> p a b", b=16)[:, 0:8, :]).then_inc(qDBG.h, 16)
            if upto in ("E", "DR", "DR1"):
                g.dma_start(out=dbg["d_x2"][:, :], in_=x2s[:, :]).then_inc(qDBG.h, 16)
                g.dma_start(out=dbg["d_ln2"].rearrange("(c p) t -> p c t", p=128),
                            in_=ln2_sb[:, :, :]).then_inc(qDBG.h, 16)
                g.dma_start(out=dbg["d_comb"][:, :], in_=combT[0:NE, :]).then_inc(qDBG.h, 16)
                g.dma_start(out=dbg["d_y"].rearrange("(c p) t -> p c t", p=128),
                            in_=y_sb[:, :, :]).then_inc(qDBG.h, 16)
        P.em("gpsimd", _)
        qDBG.inc(16 * ndbg)

    _allq = [qC, qG, qKV, qM, qQt, qXA, qKt[0], qKt[1], qKS[0], qKS[1],
             qQS[0], qQS[1], qWs[0], qWs[1], qD[0], qD[1], qXC[0], qXC[1],
             qR[0], qR[1], qO[0], qO[1], qSp, qDBG]

    def _(sy, finals=[(q, q.val) for q in _allq if q.val > 0]):
        _w(sy, finals)
    P.em("sync", _)

    P.flush()
    es.close()
    return nc



def _rope_tables(positions):
    j = np.arange(32, dtype=np.float64)
    theta = 10000.0 ** (-2.0 * j / 64.0)
    pos = (positions.astype(np.float64) + 1.0)[None, :]
    ang = theta[:, None] * pos
    ang64 = np.concatenate([ang, ang], axis=0)
    cos64 = np.cos(ang64)
    sin64 = np.sin(ang64)
    sgn = np.where(np.arange(64) < 32, -1.0, 1.0)[:, None]
    sinp = sin64 * sgn
    cos128 = np.concatenate([cos64, cos64], axis=0).astype(np.float32)
    sin128 = np.concatenate([sinp, sinp], axis=0).astype(np.float32)
    return np.ascontiguousarray(cos128), np.ascontiguousarray(sin128)


def _masks_for_core(h):
    m = np.zeros((24, 128, 512), np.float32)
    for blk in range(2):
        g = 1024 * h + 512 * blk
        for tk in range(BLK_NSLOT[blk]):
            mi = _mask_idx(blk, tk)
            if mi is None:
                continue
            ii = 128 * tk + np.arange(128)[:, None]
            jj = g + np.arange(512)[None, :]
            m[mi] = (ii <= jj).astype(np.float32)
    return m.astype(ml_dtypes.bfloat16)


def _prep_inputs(inputs):
    x = np.asarray(inputs["L0_attn_x"], np.float32)
    perm = PERM_M
    shared = dict(
        wqT=np.ascontiguousarray(np.asarray(inputs["L0_attn_Wq"], np.float32).T[:, perm]),
        bq2=np.ascontiguousarray(np.asarray(inputs["L0_attn_bq"], np.float32)[perm].reshape(HC, 128).T),
        wkT=np.ascontiguousarray(np.asarray(inputs["L0_attn_Wk"], np.float32).T),
        bk2=np.ascontiguousarray(np.asarray(inputs["L0_attn_bk"], np.float32).reshape(4, 128).T),
        wvT=np.ascontiguousarray(np.asarray(inputs["L0_attn_Wv"], np.float32).T),
        bv=np.asarray(inputs["L0_attn_bv"], np.float32),
        woT=np.ascontiguousarray(np.asarray(inputs["L0_attn_Wo"], np.float32).T[perm, :]),
        bo2=np.ascontiguousarray(np.asarray(inputs["L0_attn_bo"], np.float32).reshape(HC, 128).T),
        gateT=np.ascontiguousarray(np.asarray(inputs["L0_ffn_gate_w"], np.float32).T),
        wgu=np.ascontiguousarray(np.asarray(inputs["L0_ffn_w_gate_up"], np.float32)),
        wdn=np.ascontiguousarray(np.asarray(inputs["L0_ffn_w_down"], np.float32)),
        sgu=np.ascontiguousarray(np.asarray(inputs["L0_ffn_shared_gate_up"], np.float32)),
        sdn=np.ascontiguousarray(np.asarray(inputs["L0_ffn_shared_down"], np.float32)),
        ones1=np.ones((1, 128), np.float32),
        ones1b=np.ones((1, 128), ml_dtypes.bfloat16),
        onesr=np.ones((1, 512), ml_dtypes.bfloat16),
        onec1=np.ones((128, 1), np.float32),
        onec1b=np.ones((128, 1), ml_dtypes.bfloat16),
        identf=np.eye(128, dtype=np.float32),
    )
    onehot = np.zeros((128, 2048), np.float32)
    for e in range(NE):
        onehot[e, 128 * e:128 * e + 128] = 1.0
    shared["onehot"] = onehot.astype(ml_dtypes.bfloat16)
    shared["identb"] = np.eye(128, dtype=ml_dtypes.bfloat16)
    e0m = np.zeros((128, 128), np.float32)
    e0m[0, :] = 1.0
    shared["e0m"] = e0m.astype(ml_dtypes.bfloat16)
    shared["cosk"], shared["sink"] = _rope_tables(np.arange(SEQ))

    in_maps = []
    xT_b = [np.ascontiguousarray(x[b].T) for b in range(BATCH)]
    for c in range(NCORES):
        b, h = c // 2, c % 2
        cq, sq = _rope_tables(np.arange(1024 * h, 1024 * h + 1024))
        m = dict(shared)
        m["xT"] = xT_b[b]
        m["xTo"] = np.ascontiguousarray(xT_b[b][:, 1024 * h:1024 * h + 1024])
        m["cosq"] = cq
        m["sinq"] = sq
        m["masks"] = _masks_for_core(h)
        in_maps.append(m)
    return in_maps


def kernel(**inputs):
    in_maps = _prep_inputs(inputs)
    nc = build_program(upto="E", debug=False)
    res = run_bass_kernel_spmd(nc, in_maps, core_ids=list(range(NCORES)))
    o = np.empty((BATCH, SEQ, D), np.float32)
    for c in range(NCORES):
        b, h = c // 2, c % 2
        o[b, 1024 * h:1024 * h + 1024, :] = res.results[c]["out"].T
    return o
